# version 04
# baseline (speedup 1.0000x reference)
import numpy as np
import ml_dtypes
from contextlib import ExitStack

import concourse.bass as bass
import concourse.mybir as mybir
from concourse.bass_utils import run_bass_kernel_spmd

F32 = mybir.dt.float32
BF16 = mybir.dt.bfloat16
I32 = mybir.dt.int32
AF = mybir.ActivationFunctionType
ALU = mybir.AluOpType
AX = mybir.AxisListType

T, DM = 4096, 1024
SC, NSC = 512, 8
RH, DK, DV = 4, 256, 512
MH, HD, NOPE, RP, RLAT = 16, 64, 32, 32, 128
NE, EI = 8, 512
EPS = 1e-5
QB = [[0, 3072], [1024, 2048]]
GAMMAS = [1.0 - 2.0 ** (-5.0 - h) for h in range(RH)]
_BF = ml_dtypes.bfloat16
_LIMIT = None
_S1A_H = None
_S1A_S = None
_STG = 99
_RFIX = None
_R1 = 2
_DUMP = 0


def _host_consts():
    t = np.arange(T, dtype=np.float64)
    invf = 1.0 / (10000.0 ** (np.arange(0, DK, 2, dtype=np.float64) / DK))
    ang = invf[:, None] * t[None, :]
    cosfm = np.cos(ang).astype(np.float32).astype(_BF)
    sinfm = np.sin(ang).astype(np.float32).astype(_BF)
    i512 = np.arange(SC, dtype=np.float64)
    maskT = np.zeros((RH * SC, SC), dtype=np.float32)
    cd = np.zeros((RH * 128, SC), dtype=np.float32)
    for h in range(RH):
        g = GAMMAS[h]
        diff = i512[None, :] - i512[:, None]
        maskT[h * SC:(h + 1) * SC, :] = np.where(diff >= 0, g ** np.maximum(diff, 0.0), 0.0) / 16.0
        cd[h * 128:(h + 1) * 128, :] = g ** (i512 + 1.0) / 16.0
    kdec = np.zeros((128, 16), dtype=np.float32)
    for sub in range(4):
        for h in range(RH):
            kdec[:, sub * 4 + h] = GAMMAS[h] ** (SC - 1.0 - (sub * 128 + np.arange(128)))
    mf = 1.0 / (100000.0 ** (np.arange(0, RP, 2, dtype=np.float64) / RP))
    mang = t[:, None] * mf[None, :]
    mcos = np.cos(mang).astype(np.float32).reshape(32, 128, 16)
    msin = np.sin(mang).astype(np.float32).reshape(32, 128, 16)
    mcosq = np.concatenate([np.tile(mcos[s], (1, MH)) for s in range(32)], axis=1).astype(_BF)
    msinq = np.concatenate([np.tile(msin[s], (1, MH)) for s in range(32)], axis=1).astype(_BF)
    mcosk = np.concatenate([mcos[s] for s in range(32)], axis=1).astype(_BF)
    msink = np.concatenate([msin[s] for s in range(32)], axis=1).astype(_BF)
    tri = (np.arange(128)[None, :] >= np.arange(128)[:, None]).astype(np.float32).astype(_BF)
    idb = np.eye(128, dtype=np.float32).astype(_BF)
    idf = np.eye(128, dtype=np.float32)
    return dict(cosfm=np.asarray(cosfm), sinfm=np.asarray(sinfm), maskt=maskT,
                cd=np.asarray(cd.astype(np.float32).astype(_BF)), kdec=kdec,
                mcosq=np.asarray(mcosq), msinq=np.asarray(msinq),
                mcosk=np.asarray(mcosk), msink=np.asarray(msink),
                tri=np.asarray(tri), idb=np.asarray(idb), idf=idf)


def _perm_rope_cols(w, base, d):
    w = np.array(w, dtype=np.float32, copy=True)
    idx = np.concatenate([np.arange(base, base + d, 2), np.arange(base + 1, base + d, 2)])
    w[:, base:base + d] = w[:, idx]
    return w


def _stage_weights(ins):
    wq = np.asarray(ins["L0_attn_Wq"], np.float32).T
    wk = np.asarray(ins["L0_attn_Wk"], np.float32).T
    wv = np.asarray(ins["L0_attn_Wv"], np.float32).T
    wg = np.asarray(ins["L0_attn_Wg"], np.float32).T
    blocks = []
    for h in range(RH):
        blocks += [wq[:, h * 256:(h + 1) * 256], wk[:, h * 256:(h + 1) * 256],
                   wv[:, h * 512:(h + 1) * 512], wg[:, h * 512:(h + 1) * 512]]
    wqkvg = np.ascontiguousarray(np.concatenate(blocks, axis=1))
    wo_vf = np.ascontiguousarray(np.asarray(ins["L0_attn_Wo"], np.float32).T)
    wq_mla = np.asarray(ins["L0_ffn_Wq"], np.float32)
    for h in range(MH):
        wq_mla = _perm_rope_cols(wq_mla, h * HD + NOPE, RP)
    wq_mla = np.ascontiguousarray(wq_mla)
    wkva = np.ascontiguousarray(_perm_rope_cols(np.asarray(ins["L0_ffn_Wkva"], np.float32),
                                                RLAT, RP))
    wkvb = np.ascontiguousarray(np.asarray(ins["L0_ffn_Wkvb"], np.float32))
    wo = np.asarray(ins["L0_ffn_Wo"], np.float32)
    woeff = np.ascontiguousarray(np.concatenate([wo[h * HD:h * HD + NOPE, :] for h in range(MH)],
                                                axis=0))
    wgate = np.ascontiguousarray(np.asarray(ins["L0_ffn_Wgate"], np.float32))
    bias8 = np.ascontiguousarray(np.asarray(ins["L0_ffn_expert_bias"], np.float32).reshape(1, NE))
    gu, dn = [], []
    for e in range(NE):
        gu.append(np.concatenate([np.asarray(ins["L0_ffn_Wg"][e], np.float32),
                                  np.asarray(ins["L0_ffn_Wu"][e], np.float32)], axis=1))
        dn.append(np.asarray(ins["L0_ffn_Wd"][e], np.float32))
    gu.append(np.concatenate([np.asarray(ins["L0_ffn_sWg"][0], np.float32),
                              np.asarray(ins["L0_ffn_sWu"][0], np.float32)], axis=1))
    dn.append(np.asarray(ins["L0_ffn_sWd"][0], np.float32))
    wgu = np.ascontiguousarray(np.concatenate(gu, axis=0))
    wd = np.ascontiguousarray(np.concatenate(dn, axis=0))
    wln1 = np.ascontiguousarray(np.asarray(ins["L0_ffn_w_ln1"], np.float32).reshape(8, 128).T)
    wln2 = np.ascontiguousarray(np.asarray(ins["L0_ffn_w_ln2"], np.float32).reshape(8, 128).T)
    gnw = np.ascontiguousarray(np.asarray(ins["L0_attn_gnorm_w"], np.float32).reshape(4, 128).T)
    return dict(wqkvg=wqkvg, wo_vf=wo_vf, wq_mla=wq_mla, wkva=wkva, wkvb=wkvb,
                woeff=woeff, wgate=wgate, bias8=bias8, wgu=wgu, wd=wd,
                wln1=wln1, wln2=wln2, gnw=gnw)


class _Sem:
    def __init__(self, es, nc, name):
        self.h = es.enter_context(nc.semaphore(name))
        self.n = 0

    def inc(self, instr, k=1):
        self.n += k
        instr.then_inc(self.h, k)
        return instr

    def wait(self, eng, n=None):
        eng.wait_ge(self.h, self.n if n is None else n)


class Builder:
    def __init__(self):
        self.nc = bass.Bass("TRN2", target_bir_lowering=False)
        self.es = ExitStack()
        nc = self.nc
        self.syn, self.act, self.vec, self.ten, self.gps = (
            nc.sync, nc.scalar, nc.vector, nc.tensor, nc.gpsimd)
        self._sems = {}

    def vflush(self):
        return self.vec.tensor_copy(out=self.flv[:, 0:1], in_=self.flv[:, 1:2])

    def aflush(self):
        return self.act.copy(out=self.fla[:, 0:1], in_=self.fla[:, 1:2])

    def gflush(self):
        return self.gps.memset(self.flg[:, 0:1], 0.0)

    def sem(self, name):
        s = _Sem(self.es, self.nc, name)
        self._sems[name] = s
        return s

    def fence(self):
        for eng in (self.syn, self.act, self.vec, self.ten, self.gps):
            for s in self._sems.values():
                eng.wait_ge(s.h, s.n)

    def sbuf(self, name, shape, dt, scope=None, **kw):
        cm = self.nc.sbuf_tensor(name, shape, dt, **kw)
        return (scope or self.es).enter_context(cm)

    def psum(self, name, shape, dt, scope=None):
        cm = self.nc.psum_tensor(name, shape, dt)
        return (scope or self.es).enter_context(cm)

    def build(self):
        nc = self.nc
        din = lambda n, s, d: nc.dram_tensor(n, s, d, kind="ExternalInput").ap()
        dint = lambda n, s, d: nc.dram_tensor(n, s, d, kind="Internal").ap()
        self.x0d = din("x0", [T, DM], F32)
        self.roled = din("role", [1, 1], I32)
        self.wqkvgd = din("wqkvg", [DM, RH * 1536], F32)
        self.wo_vfd = din("wo_vf", [2048, DM], F32)
        self.wqmlad = din("wq_mla", [DM, DM], F32)
        self.wkvad = din("wkva", [DM, 160], F32)
        self.wkvbd = din("wkvb", [RLAT, 512], F32)
        self.woeffd = din("woeff", [512, DM], F32)
        self.wgated = din("wgate", [DM, NE], F32)
        self.bias8d = din("bias8", [1, NE], F32)
        self.wgud = din("wgu", [9 * DM, DM], F32)
        self.wdd = din("wd", [9 * EI, DM], F32)
        self.wln1d = din("wln1", [128, 8], F32)
        self.wln2d = din("wln2", [128, 8], F32)
        self.gnwd = din("gnw", [128, 4], F32)
        self.cosfmd = din("cosfm", [128, T], BF16)
        self.sinfmd = din("sinfm", [128, T], BF16)
        self.masktd = din("maskt", [RH * SC, SC], F32)
        self.cdd = din("cd", [RH * 128, SC], BF16)
        self.kdecd = din("kdec", [128, 16], F32)
        self.mcosqd = din("mcosq", [128, 32 * 256], BF16)
        self.msinqd = din("msinq", [128, 32 * 256], BF16)
        self.mcoskd = din("mcosk", [128, 512], BF16)
        self.msinkd = din("msink", [128, 512], BF16)
        self.trid = din("tri", [128, 128], BF16)
        self.idbd = din("idb", [128, 128], BF16)
        self.idfd = din("idf", [128, 128], F32)

        self.ogd = dint("og_s", [2048, T], BF16)
        self.x1d = dint("x1_s", [T, DM], F32)
        self.std_ = dint("st_s", [T, 4], F32)
        self.qd = dint("q_s", [T, DM], BF16)
        self.y1d = dint("y1_s", [2048, DM], F32)
        self.x1cd = dint("x1c_s", [2048, DM], F32)
        self.outd = nc.dram_tensor("out", [2048, DM], F32, kind="ExternalOutput").ap()
        if _DUMP:
            self.dbgd = dint("dbg_s", [2048, T], BF16)
            self.dbg2d = dint("dbg2_s", [2048, T], BF16)
            self.dbg3d = dint("dbg3_s", [128, 32 * 528], BF16)
            self.dbg4d = dint("dbg4_s", [128, 64], F32)
            self.dbg5d = dint("dbg5_s", [128, 4096], BF16)

        self.srole = self.sbuf("srole", [1, 1], I32)
        self.flv = self.sbuf("flv", [128, 2], F32)
        self.fla = self.sbuf("fla", [128, 2], F32)
        self.flg = self.sbuf("flg", [128, 2], F32)
        self.cst = self.sbuf("cst", [128, 4], F32)
        self.wln = self.sbuf("wln", [128, 16], F32)
        self.gnw = self.sbuf("gnw_sb", [128, 4], F32)
        self.kdec = self.sbuf("kdec_sb", [128, 16], F32)
        self.tri = self.sbuf("tri_sb", [128, 128], BF16)
        self.idb = self.sbuf("idb_sb", [128, 128], BF16)
        self.idf = self.sbuf("idf_sb", [128, 128], F32)
        self.ones_f = self.sbuf("ones_f", [1, 128], F32)
        self.ones_b = self.sbuf("ones_b", [1, 128], BF16)
        self.bias_all = self.sbuf("bias_all", [128, 16], F32)
        self.biasg = self.sbuf("biasg", [128, 8], F32)
        self.b8row = self.sbuf("b8row", [1, 8], F32)
        self.mck = self.sbuf("mck", [128, 512], BF16)
        self.msk = self.sbuf("msk", [128, 512], BF16)
        self.nmax = self.sbuf("nmax", [128, 36], F32)
        self.stat = self.sbuf("stat", [128, 16], F32)

        self.ds = self.sem("ds")
        self.dg = self.sem("dg")
        self.gp = self.sem("gp")
        self.dmp = self.sem("dmp") if _DUMP else None

        self.p0()
        done = False
        with ExitStack() as z_scope:
            self.zfm = self.sbuf("zfm", [128, 8 * T], BF16, scope=z_scope,
                                 is_transpose_dest=True)
            self.cosfm = self.sbuf("cosfm_sb", [128, T], BF16, scope=z_scope)
            self.sinfm = self.sbuf("sinfm_sb", [128, T], BF16, scope=z_scope)
            self.prepass()
            if _DUMP:
                self.dz.wait(self.syn)
                for ci in range(8):
                    self.dmp.inc(self.syn.dma_start(
                        out=self.dbgd[ci * 128:(ci + 1) * 128, :],
                        in_=self.zfm[:, ci * T:(ci + 1) * T]), 16)
            if _LIMIT == 1:
                done = True
            if not done:
                self.s1a()
                if _LIMIT == 2:
                    done = True
        if not done:
            self.s1b()
            if _LIMIT == 3:
                done = True
        if not done:
            with ExitStack() as kv_scope:
                self.v_tm = self.sbuf("v_tm", [128, 32 * 528], BF16, scope=kv_scope)
                self.k_fm = self.sbuf("k_fm", [128, 8 * T], BF16, scope=kv_scope)
                self.p2a()
                if _LIMIT == 4:
                    done = True
                if not done:
                    self.p2b()
                    if _LIMIT == 5:
                        done = True
        if not done:
            self.p3()
        else:
            self.debug_out()
        if _DUMP:
            self.dmp.wait(self.syn)
        self.es.close()
        return self.nc

    def debug_out(self):
        syn = self.syn
        dfin = self.sem("dfin")
        self.fence()
        with self.nc.sbuf_tensor("dumout", [128, DM], F32) as du:
            self.gps.memset(du[:, :], 0.0)
            self.gp.inc(self.gflush())
            self.gp.wait(syn)
            for s in range(16):
                dfin.inc(syn.dma_start(out=self.outd[s * 128:(s + 1) * 128, :],
                                       in_=du[:, :]), 16)
            dfin.wait(syn)

    def p0(self):
        syn, gps = self.syn, self.gps
        for dst, src in [(self.srole[:], self.roled), (self.wln[:, 0:8], self.wln1d),
                         (self.wln[:, 8:16], self.wln2d), (self.gnw[:], self.gnwd),
                         (self.kdec[:], self.kdecd), (self.tri[:], self.trid),
                         (self.idb[:], self.idbd), (self.idf[:], self.idfd), (self.mck[:], self.mcoskd),
                         (self.msk[:], self.msinkd), (self.b8row[:], self.bias8d)]:
            self.ds.inc(syn.dma_start(out=dst, in_=src), 16)
        gps.memset(self.cst[:, 0:1], EPS)
        gps.memset(self.cst[:, 1:2], 1.0)
        gps.memset(self.ones_f[:, :], 1.0)
        gps.memset(self.ones_b[:, :], 1.0)
        gps.memset(self.nmax[:, :], 0.0)
        self.gp.inc(self.gflush())

    def prepass(self):
        nc, syn, act, vec = self.nc, self.syn, self.act, self.vec
        self.dz = self.sem("dz")
        ve0, ac0 = self.sem("ve0"), self.sem("ac0")
        stat, cst = self.stat, self.cst
        ds = self.ds
        ds.inc(syn.dma_start(out=self.cosfm[:], in_=self.cosfmd), 16)
        ds.inc(syn.dma_start(out=self.sinfm[:], in_=self.sinfmd), 16)
        with nc.sbuf_tensor("x0b", [128, 2 * DM], F32) as x0b, \
             nc.sbuf_tensor("ztm", [128, 2 * DM], BF16) as ztm, \
             nc.sbuf_tensor("sqs", [128, DM], BF16) as sqs:
            for it in range(T // 128):
                b = it % 2
                if it >= 2:
                    ac0.wait(syn, 2 * (it - 1))
                    ve0.wait(syn, 2 * (it - 1))
                ds.inc(syn.dma_start(out=x0b[:, b * DM:(b + 1) * DM],
                                     in_=self.x0d[it * 128:(it + 1) * 128, :]), 16)
                ds.wait(act)
                self.gp.wait(act)
                if it >= 1:
                    ve0.wait(act, 2 * it - 1)
                ac0.inc(act.activation(out=sqs[:, :], in_=x0b[:, b * DM:(b + 1) * DM],
                                       func=AF.Square, accum_out=stat[:, 0:1]))
                ds.wait(vec)
                vec.reduce_sum(out=stat[:, 1:2], in_=x0b[:, b * DM:(b + 1) * DM], axis=AX.X)
                self.vflush()
                vec.tensor_scalar(out=stat[:, 1:2], in0=stat[:, 1:2], scalar1=1.0 / DM,
                                  scalar2=None, op0=ALU.mult)
                ac0.wait(vec)
                vec.tensor_scalar(out=stat[:, 0:1], in0=stat[:, 0:1], scalar1=1.0 / DM,
                                  scalar2=None, op0=ALU.mult)
                vec.tensor_tensor(out=stat[:, 2:3], in0=stat[:, 1:2], in1=stat[:, 1:2],
                                  op=ALU.mult)
                self.vflush()
                ve0.inc(vec.tensor_tensor(out=stat[:, 0:1], in0=stat[:, 0:1],
                                          in1=stat[:, 2:3], op=ALU.subtract))
                ve0.wait(act)
                act.activation(out=stat[:, 2:3], in_=stat[:, 0:1], func=AF.Ln, bias=cst[:, 0:1])
                ac0.inc(act.activation(out=stat[:, 2:3], in_=stat[:, 2:3], func=AF.Exp,
                                       scale=-0.5))
                ac0.wait(vec)
                if it >= 2:
                    self.dz.wait(vec, 16 * 8 * (it - 1))
                ve0.inc(vec.tensor_scalar(out=ztm[:, b * DM:(b + 1) * DM],
                                          in0=x0b[:, b * DM:(b + 1) * DM],
                                          scalar1=stat[:, 1:2], scalar2=stat[:, 2:3],
                                          op0=ALU.subtract, op1=ALU.mult))
                ve0.wait(syn)
                for ci in range(8):
                    self.dz.inc(syn.dma_start_transpose(
                        out=self.zfm[:, ci * T + it * 128:ci * T + (it + 1) * 128],
                        in_=ztm[:, b * DM + ci * 128:b * DM + (ci + 1) * 128]), 16)

    def s1a(self):
        if _STG != 0:
            self.fence()
        nc = self.nc
        syn, act, vec, ten, gps = self.syn, self.act, self.vec, self.ten, self.gps
        zfm, kdec, stat, cst = self.zfm, self.kdec, self.stat, self.cst

        pqk, pvgs, pat, po, psd = (self.sem("pqk"), self.sem("pvgs"), self.sem("pat"),
                                   self.sem("po"), self.sem("psd"))
        vqk, vrp, vkd, vvv, vat, vog, vsb = (self.sem("vqk"), self.sem("vrp"), self.sem("vkd"),
                                             self.sem("vvv"), self.sem("vat"), self.sem("vog"),
                                             self.sem("vsb"))
        ag, aor, ass = self.sem("ag"), self.sem("aor"), self.sem("ass")
        dktm, dogt = self.sem("dktm"), self.sem("dogt")
        self.dog = self.sem("dog")
        dog = self.dog
        ds, dg = self.ds, self.dg

        _st = ExitStack()
        s1w = _st.enter_context(nc.sbuf_tensor("s1w", [128, 8 * 1536], BF16))
        maskt = _st.enter_context(nc.sbuf_tensor("maskt_sb", [128, 4 * SC], F32))
        cdt = _st.enter_context(nc.sbuf_tensor("cdt", [128, SC], BF16))
        qkpre = _st.enter_context(nc.sbuf_tensor("qkpre", [128, 4 * SC], BF16))
        qrp = _st.enter_context(nc.sbuf_tensor("qrp", [128, 2 * SC], BF16))
        krp = _st.enter_context(nc.sbuf_tensor("krp", [128, 2 * SC], BF16))
        qtl = _st.enter_context(nc.sbuf_tensor("qtl", [128, 2 * SC], BF16))
        rsc = _st.enter_context(nc.sbuf_tensor("rsc", [128, 2 * SC], BF16))
        ktm = _st.enter_context(nc.sbuf_tensor("ktm", [128, 4 * DK], BF16, is_transpose_dest=True))
        kdk = _st.enter_context(nc.sbuf_tensor("kdk", [128, 4 * DK], BF16))
        vtm1 = _st.enter_context(nc.sbuf_tensor("vtm1", [128, 4 * DV], BF16))
        gsil = _st.enter_context(nc.sbuf_tensor("gsil", [128, 4 * DV], BF16))
        att = _st.enter_context(nc.sbuf_tensor("att", [128, 4 * SC], BF16))
        ogtm = _st.enter_context(nc.sbuf_tensor("ogtm", [128, 4 * DV], BF16))
        ogfm = _st.enter_context(nc.sbuf_tensor("ogfm", [128, 4 * SC], BF16, is_transpose_dest=True))
        o2s = _st.enter_context(nc.sbuf_tensor("o2s", [128, DV], BF16))
        og1 = _st.enter_context(nc.sbuf_tensor("og1", [128, DV], BF16))
        ssb = _st.enter_context(nc.sbuf_tensor("ssb", [128, 2 * DV], F32))
        sbf = _st.enter_context(nc.sbuf_tensor("sbf", [128, 2 * DV], BF16))
        pq = _st.enter_context(nc.psum_tensor("pq", [128, 2 * SC], F32))
        pk = _st.enter_context(nc.psum_tensor("pk", [128, 2 * SC], F32))
        pvg = _st.enter_context(nc.psum_tensor("pvg", [128, 4 * SC], F32))
        if True:

            Wq = lambda ci, dc: ci * 1536 + dc * 128
            Wk = lambda ci, dc: ci * 1536 + 256 + dc * 128
            Wv = lambda ci: ci * 1536 + 512
            Wg = lambda ci: ci * 1536 + 1024
            dwh, dsh = [], []

            for h in range(_S1A_H if _S1A_H else RH):
                if h > 0 and _STG >= 7:
                    psd.wait(gps, h * NSC)
                for ci in range(8):
                    dg.inc(gps.dma_start(out=s1w[:, ci * 1536:(ci + 1) * 1536],
                                         in_=self.wqkvgd[ci * 128:(ci + 1) * 128,
                                                         h * 1536:(h + 1) * 1536]), 16)
                dwh.append(dg.n)
                if h > 0 and _STG >= 4:
                    vat.wait(syn, 4 * h * NSC)
                if h > 0 and _STG >= 5:
                    po.wait(syn, 4 * h * NSC)
                for j in range(4):
                    ds.inc(syn.dma_start(out=maskt[:, j * SC:(j + 1) * SC],
                                         in_=self.masktd[h * SC + j * 128:h * SC + (j + 1) * 128, :]),
                           16)
                ds.inc(syn.dma_start(out=cdt[:], in_=self.cdd[h * 128:(h + 1) * 128, :]), 16)
                dsh.append(ds.n)
                gC = float(GAMMAS[h] ** SC)

                for sci in range((_S1A_S if _S1A_S else NSC) if _STG >= 1 else 0):
                    it = h * NSC + sci
                    tok0 = sci * SC
                    dg.wait(ten, dwh[h])
                    if it == 0:
                        self.dz.wait(ten, 16 * 8 * (T // 128))
                        self.gp.wait(ten)
                        ds.wait(ten, dsh[0])
                    else:
                        vqk.wait(ten, 4 * it)
                        if _STG >= 4:
                            vat.wait(ten, 4 * it)
                        if _STG >= 7:
                            vsb.wait(ten, it)
                    for dc in range(2):
                        for ci in range(8):
                            ten.matmul(pq[:, dc * SC:(dc + 1) * SC],
                                       s1w[:, Wq(ci, dc):Wq(ci, dc) + 128],
                                       zfm[:, ci * T + tok0:ci * T + tok0 + SC],
                                       start=(ci == 0), stop=(ci == 7))
                    for dc in range(2):
                        for ci in range(8):
                            mm_ = ten.matmul(pk[:, dc * SC:(dc + 1) * SC],
                                             s1w[:, Wk(ci, dc):Wk(ci, dc) + 128],
                                             zfm[:, ci * T + tok0:ci * T + tok0 + SC],
                                             start=(ci == 0), stop=(ci == 7))
                    pqk.inc(mm_)
                    for sub in range(4 if _STG >= 2 else 0):
                        bk = sub % 2
                        if sub < 2:
                            if it >= 1 and _STG >= 5:
                                vog.wait(ten, 4 * (it - 1) + 2 * (sub + 1))
                                aor.wait(ten, 4 * (it - 1) + 2 * (sub + 1))
                        else:
                            vvv.wait(ten, 4 * it + sub - 1)
                            ag.wait(ten, 4 * it + sub - 1)
                        for ci in range(8):
                            ten.matmul(pvg[:, (2 * bk) * SC:(2 * bk + 1) * SC],
                                       zfm[:, ci * T + tok0 + sub * 128:ci * T + tok0 + (sub + 1) * 128],
                                       s1w[:, Wv(ci):Wv(ci) + 512],
                                       start=(ci == 0), stop=(ci == 7))
                        for ci in range(8):
                            mm_ = ten.matmul(pvg[:, (2 * bk + 1) * SC:(2 * bk + 2) * SC],
                                             zfm[:, ci * T + tok0 + sub * 128:ci * T + tok0 + (sub + 1) * 128],
                                             s1w[:, Wg(ci):Wg(ci) + 512],
                                             start=(ci == 0), stop=(ci == 7))
                        pvgs.inc(mm_)
                    for j in range(4 if _STG >= 4 else 0):
                        bk = j % 2
                        if j < 2:
                            vqk.wait(ten, 4 * it + 2)
                            vrp.wait(ten, it + 1)
                        else:
                            vat.wait(ten, 4 * it + j - 1)
                        for dc in range(2):
                            mm_ = ten.matmul(pq[:, bk * SC:(bk + 1) * SC],
                                             krp[:, dc * SC + j * 128:dc * SC + (j + 1) * 128],
                                             qrp[:, dc * SC:(dc + 1) * SC],
                                             start=(dc == 0), stop=(dc == 1))
                        pat.inc(mm_)
                    for ii in range(4 if _STG >= 5 else 0):
                        vat.wait(ten, 4 * it + 4)
                        if ii < 2:
                            vvv.wait(ten, 4 * it + 3)
                            ag.wait(ten, 4 * it + 3)
                            if _STG >= 7:
                                vsb.wait(ten, it)
                        else:
                            vvv.wait(ten, 4 * it + 4)
                            ag.wait(ten, 4 * it + 4)
                        for j in range(4):
                            mm_ = ten.matmul(pvg[:, ii * SC:(ii + 1) * SC],
                                             att[:, j * SC + ii * 128:j * SC + (ii + 1) * 128],
                                             vtm1[:, j * DV:(j + 1) * DV],
                                             start=(j == 0),
                                             stop=(j == 3 and (sci == 0 or _STG < 7)))
                        if sci > 0 and _STG >= 7:
                            for dc in range(2):
                                mm_ = ten.matmul(pvg[:, ii * SC:(ii + 1) * SC],
                                                 qtl[:, dc * SC + ii * 128:dc * SC + (ii + 1) * 128],
                                                 sbf[:, dc * DV:(dc + 1) * DV],
                                                 start=False, stop=(dc == 1))
                        po.inc(mm_)
                    vqk.wait(ten, 4 * it + 4) if _STG >= 7 else None
                    vkd.wait(ten, it + 1) if _STG >= 7 else None
                    vvv.wait(ten, 4 * it + 4) if _STG >= 7 else None
                    for st_ in range(2 if _STG >= 7 else 0):
                        for sub in range(4):
                            mm_ = ten.matmul(pk[:, st_ * SC:(st_ + 1) * SC],
                                             kdk[:, sub * DK + st_ * 128:sub * DK + (st_ + 1) * 128],
                                             vtm1[:, sub * DV:(sub + 1) * DV],
                                             start=(sub == 0), stop=(sub == 3))
                    psd.inc(mm_) if _STG >= 7 else None

                    pqk.wait(vec, it + 1)
                    if it >= 1:
                        if _STG >= 4:
                            pat.wait(vec, 4 * it)
                        if _STG >= 3:
                            dktm.wait(vec, 128 * it)
                        if _STG >= 5:
                            po.wait(vec, 4 * it)
                        if _STG >= 7:
                            psd.wait(vec, it)
                    vqk.inc(vec.tensor_copy(out=qkpre[:, 0:SC], in_=pq[:, 0:SC]))
                    vqk.inc(vec.tensor_copy(out=qkpre[:, SC:2 * SC], in_=pq[:, SC:2 * SC]))
                    vqk.inc(vec.tensor_copy(out=qkpre[:, 2 * SC:3 * SC], in_=pk[:, 0:SC]))
                    vqk.inc(vec.tensor_copy(out=qkpre[:, 3 * SC:4 * SC], in_=pk[:, SC:2 * SC]))
                    cosw = self.cosfm[:, tok0:tok0 + SC]
                    sinw = self.sinfm[:, tok0:tok0 + SC]
                    for (s0, s1_, dst) in ([(qkpre[:, 0:SC], qkpre[:, SC:2 * SC], qrp),
                                            (qkpre[:, 2 * SC:3 * SC], qkpre[:, 3 * SC:4 * SC], krp)]
                                           if _R1 >= 1 else []):
                        vec.tensor_tensor(out=rsc[:, 0:SC], in0=s0, in1=cosw, op=ALU.mult)
                        vec.tensor_tensor(out=rsc[:, SC:2 * SC], in0=s1_, in1=sinw, op=ALU.mult)
                        vec.tensor_tensor(out=rsc[:, 0:SC], in0=rsc[:, 0:SC],
                                          in1=rsc[:, SC:2 * SC], op=ALU.subtract)
                        vec.tensor_tensor(out=rsc[:, SC:2 * SC], in0=s1_, in1=cosw, op=ALU.mult)
                        vec.tensor_tensor(out=dst[:, SC:2 * SC], in0=s0, in1=sinw, op=ALU.mult)
                        vec.tensor_tensor(out=dst[:, SC:2 * SC], in0=dst[:, SC:2 * SC],
                                          in1=rsc[:, SC:2 * SC], op=ALU.add)
                        vec.tensor_copy(out=dst[:, 0:SC], in_=rsc[:, 0:SC])
                    ds.wait(vec, dsh[h])
                    for dc in range(2 if _R1 >= 2 else 0):
                        vec.tensor_tensor(out=qtl[:, dc * SC:(dc + 1) * SC],
                                          in0=qrp[:, dc * SC:(dc + 1) * SC], in1=cdt[:],
                                          op=ALU.mult)
                    vrp.inc(self.vflush())
                    dktm.wait(vec, 128 * (it + 1)) if _STG >= 3 else None
                    for sub in range(4 if _STG >= 3 else 0):
                        vec.tensor_scalar(out=kdk[:, sub * DK:(sub + 1) * DK],
                                          in0=ktm[:, sub * DK:(sub + 1) * DK],
                                          scalar1=kdec[:, sub * 4 + h:sub * 4 + h + 1],
                                          scalar2=None, op0=ALU.mult)
                    vkd.inc(self.vflush()) if _STG >= 3 else None
                    for sub in range(4 if _STG >= 2 else 0):
                        bk = sub % 2
                        pvgs.wait(vec, 4 * it + sub + 1)
                        vvv.inc(vec.tensor_copy(out=vtm1[:, sub * DV:(sub + 1) * DV],
                                                in_=pvg[:, 2 * bk * SC:(2 * bk + 1) * SC]))
                    for j in range(4 if _STG >= 4 else 0):
                        bk = j % 2
                        pat.wait(vec, 4 * it + j + 1)
                        vat.inc(vec.tensor_tensor(out=att[:, j * SC:(j + 1) * SC],
                                                  in0=pq[:, bk * SC:(bk + 1) * SC],
                                                  in1=maskt[:, j * SC:(j + 1) * SC], op=ALU.mult))
                    if it >= 1 and _STG >= 6:
                        dogt.wait(vec, 256 * it)
                    for ii in range(4 if _STG >= 5 else 0):
                        aor.wait(vec, 4 * it + ii + 1)
                        pc = 9 + (ii % 2) * 2
                        vec.tensor_scalar(out=og1[:, :], in0=pvg[:, ii * SC:(ii + 1) * SC],
                                          scalar1=stat[:, pc:pc + 1], scalar2=None, op0=ALU.mult)
                        vog.inc(vec.tensor_tensor(out=ogtm[:, ii * DV:(ii + 1) * DV],
                                                  in0=og1[:, :], in1=gsil[:, ii * DV:(ii + 1) * DV],
                                                  op=ALU.mult))
                    if _STG >= 7:
                        ass.wait(vec, 2 * (it + 1))
                        psd.wait(vec, it + 1)
                        if sci == 0:
                            vec.tensor_copy(out=ssb[:, 0:DV], in_=pk[:, 0:SC])
                            vec.tensor_copy(out=ssb[:, DV:2 * DV], in_=pk[:, SC:2 * SC])
                        else:
                            vec.tensor_tensor(out=ssb[:, 0:DV], in0=ssb[:, 0:DV],
                                              in1=pk[:, 0:SC], op=ALU.add)
                            vec.tensor_tensor(out=ssb[:, DV:2 * DV], in0=ssb[:, DV:2 * DV],
                                              in1=pk[:, SC:2 * SC], op=ALU.add)
                        vec.tensor_copy(out=sbf[:, 0:DV], in_=ssb[:, 0:DV])
                        vec.tensor_copy(out=sbf[:, DV:2 * DV], in_=ssb[:, DV:2 * DV])
                    vsb.inc(self.vflush())

                    if _STG >= 3:
                        vrp.wait(act, it + 1)
                    if it >= 1 and _STG >= 3:
                        vkd.wait(act, it)
                    for dc in range(2 if _STG >= 3 else 0):
                        eng_ = act
                        for sub in range(4):
                            dktm.inc(eng_.dma_start_transpose(
                                out=ktm[:, sub * DK + dc * 128:sub * DK + (dc + 1) * 128],
                                in_=krp[:, dc * SC + sub * 128:dc * SC + (sub + 1) * 128]), 16)
                    if it >= 1 and _STG >= 5:
                        vog.wait(act, 4 * it)
                    for sub in range(4 if _STG >= 2 else 0):
                        bk = sub % 2
                        pvgs.wait(act, 4 * it + sub + 1)
                        ag.inc(act.activation(out=gsil[:, sub * DV:(sub + 1) * DV],
                                              in_=pvg[:, (2 * bk + 1) * SC:(2 * bk + 2) * SC],
                                              func=AF.Silu))
                    for ii in range(4 if _STG >= 5 else 0):
                        po.wait(act, 4 * it + ii + 1)
                        if 4 * it + ii >= 2:
                            vog.wait(act, 4 * it + ii - 1)
                        pc = 9 + (ii % 2) * 2
                        act.activation(out=o2s[:, :], in_=pvg[:, ii * SC:(ii + 1) * SC],
                                       func=AF.Square, accum_out=stat[:, pc - 1:pc])
                        act.activation(out=stat[:, pc:pc + 1], in_=stat[:, pc - 1:pc],
                                       func=AF.Ln, scale=1.0 / DV, bias=cst[:, 0:1])
                        aor.inc(act.activation(out=stat[:, pc:pc + 1], in_=stat[:, pc:pc + 1],
                                               func=AF.Exp, scale=-0.5))
                    if _STG >= 6:
                        vog.wait(act, 4 * it + 4)
                    if it >= 1 and _STG >= 6:
                        dog.wait(act, 64 * it)
                    for vc in range(4 if _STG >= 6 else 0):
                        eng_ = act
                        for tt in range(4):
                            dogt.inc(eng_.dma_start_transpose(
                                out=ogfm[:, vc * SC + tt * 128:vc * SC + (tt + 1) * 128],
                                in_=ogtm[:, tt * DV + vc * 128:tt * DV + (vc + 1) * 128]), 16)
                    if sci == 0 or _STG < 7:
                        if _STG >= 7:
                            ass.inc(self.aflush())
                            ass.inc(self.aflush())
                    else:
                        vsb.wait(act, it)
                        ass.inc(act.activation(out=ssb[:, 0:DV], in_=ssb[:, 0:DV],
                                                func=AF.Copy, scale=gC))
                        ass.inc(act.activation(out=ssb[:, DV:2 * DV], in_=ssb[:, DV:2 * DV],
                                               func=AF.Copy, scale=gC))

                    if _STG >= 6:
                        dogt.wait(syn, 256 * (it + 1))
                    for vc in range(4 if _STG >= 6 else 0):
                        dog.inc(syn.dma_start(
                            out=self.ogd[h * 512 + vc * 128:h * 512 + (vc + 1) * 128,
                                         tok0:tok0 + SC],
                            in_=ogfm[:, vc * SC:(vc + 1) * SC]), 16)

            self.s1a_tot = dict(psd=psd.n, vsb=vsb.n, dog=dog.n, vat=vat.n, po=po.n,
                                aor=aor.n, vog=vog.n, ag=ag.n, vvv=vvv.n, vqk=vqk.n,
                                pat=pat.n, pqk=pqk.n, pvgs=pvgs.n, ass=ass.n,
                                vkd=vkd.n, vrp=vrp.n, dktm=dktm.n, dogt=dogt.n)
        _st.close()

    def s1b(self):
        self.fence()
        nc = self.nc
        syn, act, vec, ten, gps = self.syn, self.act, self.vec, self.ten, self.gps
        stat, cst = self.stat, self.cst
        ds, dg = self.ds, self.dg
        pr_s = self.sem("pr1b")
        vr_s = self.sem("vr1b")
        ar_s = self.sem("ar1b")
        dx_s = self.sem("dx1b")
        dsp = self.sem("dsp1b")
        dgw = self.sem("dgw1b")
        vwo = self.sem("vwo1b")

        with nc.sbuf_tensor("wo_sb", [128, 16 * DM], BF16) as wo_sb, \
             nc.sbuf_tensor("ogl", [128, 2 * 16 * SC], BF16) as ogl, \
             nc.sbuf_tensor("x0c", [128, 2 * DM], F32) as x0c, \
             nc.sbuf_tensor("rsb", [128, 2 * DM], F32) as rsb, \
             nc.sbuf_tensor("x1sb", [128, 2 * DM], F32) as x1sb, \
             nc.sbuf_tensor("sq1", [128, DM], BF16) as sq1, \
             nc.psum_tensor("pr", [128, 4 * SC], F32) as pr:

            self.psd_final = self.s1a_tot["psd"]
            for vc in range(16):
                dgw.inc(gps.dma_start(out=wo_sb[:, vc * DM:(vc + 1) * DM],
                                      in_=self.wo_vfd[vc * 128:(vc + 1) * 128, :]), 16)
            dgw.wait(vec)
            for vc in range(16):
                vec.tensor_scalar(out=wo_sb[:, vc * DM:(vc + 1) * DM],
                                  in0=wo_sb[:, vc * DM:(vc + 1) * DM],
                                  scalar1=self.gnw[:, vc % 4:vc % 4 + 1], scalar2=None,
                                  op0=ALU.mult)
            vwo.inc(self.vflush())

            vr_land = []
            ar_land = []
            for tt in range(T // 128):
                b = tt % 2
                ch = tt // 4
                if tt % 4 == 0:
                    bb = ch % 2
                    if tt == 0:
                        self.dog.wait(syn, self.s1a_tot["dog"])
                    if ch >= 2:
                        pr_s.wait(syn, 4 * (ch - 1))
                    for vc in range(16):
                        dx_s.inc(syn.dma_start(
                            out=ogl[:, bb * 16 * SC + vc * SC:bb * 16 * SC + (vc + 1) * SC],
                            in_=self.ogd[vc * 128:(vc + 1) * 128, ch * SC:(ch + 1) * SC]), 16)
                if tt >= 2:
                    vr_s.wait(syn, vr_land[tt - 2])
                dx_s.inc(syn.dma_start(out=x0c[:, b * DM:(b + 1) * DM],
                                       in_=self.x0d[tt * 128:(tt + 1) * 128, :]), 16)
                bb = ch % 2
                sub = tt % 4
                dx_s.wait(ten)
                vwo.wait(ten)
                if tt >= 2:
                    vr_s.wait(ten, vr_land[tt - 2])
                for fh in range(2):
                    for vc in range(16):
                        mm_ = ten.matmul(pr[:, (2 * b + fh) * SC:(2 * b + fh + 1) * SC],
                                         ogl[:, bb * 16 * SC + vc * SC + sub * 128:
                                             bb * 16 * SC + vc * SC + (sub + 1) * 128],
                                         wo_sb[:, vc * DM + fh * SC:vc * DM + (fh + 1) * SC],
                                         start=(vc == 0), stop=(vc == 15))
                pr_s.inc(mm_)
                pr_s.wait(vec, tt + 1)
                vec.tensor_copy(out=rsb[:, b * DM:b * DM + SC], in_=pr[:, 2 * b * SC:(2 * b + 1) * SC])
                vr_s.inc(vec.tensor_copy(out=rsb[:, b * DM + SC:(b + 1) * DM],
                                         in_=pr[:, (2 * b + 1) * SC:(2 * b + 2) * SC]))
                vr_s.wait(act)
                ar_s.inc(act.activation(out=sq1[:, :], in_=rsb[:, b * DM:(b + 1) * DM],
                                        func=AF.Square, accum_out=stat[:, 0:1]))
                vec.reduce_sum(out=stat[:, 1:2], in_=rsb[:, b * DM:(b + 1) * DM], axis=AX.X)
                self.vflush()
                vec.tensor_scalar(out=stat[:, 1:2], in0=stat[:, 1:2], scalar1=1.0 / DM,
                                  scalar2=None, op0=ALU.mult)
                ar_s.wait(vec)
                vec.tensor_scalar(out=stat[:, 0:1], in0=stat[:, 0:1], scalar1=1.0 / DM,
                                  scalar2=None, op0=ALU.mult)
                vec.tensor_tensor(out=stat[:, 2:3], in0=stat[:, 1:2], in1=stat[:, 1:2],
                                  op=ALU.mult)
                self.vflush()
                vr_s.inc(vec.tensor_tensor(out=stat[:, 0:1], in0=stat[:, 0:1], in1=stat[:, 2:3],
                                           op=ALU.subtract))
                vr_s.wait(act)
                act.activation(out=stat[:, 2:3], in_=stat[:, 0:1], func=AF.Ln, bias=cst[:, 0:1])
                ar_s.inc(act.activation(out=stat[:, 2:3], in_=stat[:, 2:3], func=AF.Exp,
                                        scale=-0.5))
                ar_s.wait(vec)
                vec.tensor_scalar(out=x1sb[:, b * DM:(b + 1) * DM], in0=rsb[:, b * DM:(b + 1) * DM],
                                  scalar1=stat[:, 1:2], scalar2=stat[:, 2:3],
                                  op0=ALU.subtract, op1=ALU.mult)
                vr_s.inc(vec.tensor_tensor(out=x1sb[:, b * DM:(b + 1) * DM],
                                           in0=x1sb[:, b * DM:(b + 1) * DM],
                                           in1=x0c[:, b * DM:(b + 1) * DM], op=ALU.add))
                vr_s.wait(syn)
                dsp.inc(syn.dma_start(out=self.x1d[tt * 128:(tt + 1) * 128, :],
                                      in_=x1sb[:, b * DM:(b + 1) * DM]), 16)
                vr_s.wait(act)
                ar_s.inc(act.activation(out=sq1[:, :], in_=x1sb[:, b * DM:(b + 1) * DM],
                                        func=AF.Square, accum_out=stat[:, 0:1]))
                vec.reduce_sum(out=stat[:, 1:2], in_=x1sb[:, b * DM:(b + 1) * DM], axis=AX.X)
                self.vflush()
                vec.tensor_scalar(out=stat[:, 1:2], in0=stat[:, 1:2], scalar1=1.0 / DM,
                                  scalar2=None, op0=ALU.mult)
                ar_s.wait(vec)
                vec.tensor_scalar(out=stat[:, 0:1], in0=stat[:, 0:1], scalar1=1.0 / DM,
                                  scalar2=None, op0=ALU.mult)
                vec.tensor_tensor(out=stat[:, 2:3], in0=stat[:, 1:2], in1=stat[:, 1:2],
                                  op=ALU.mult)
                self.vflush()
                vr_s.inc(vec.tensor_tensor(out=stat[:, 0:1], in0=stat[:, 0:1],
                                           in1=stat[:, 2:3], op=ALU.subtract))
                vr_s.wait(act)
                act.activation(out=stat[:, 3:4], in_=stat[:, 0:1], func=AF.Ln, bias=cst[:, 0:1])
                ar_s.inc(act.activation(out=stat[:, 3:4], in_=stat[:, 3:4], func=AF.Exp,
                                        scale=-0.5))
                ar_s.wait(vec)
                vec.tensor_scalar(out=stat[:, 5:6], in0=stat[:, 0:1], scalar1=cst[:, 0:1],
                                  scalar2=None, op0=ALU.add)
                self.vflush()
                vec.reciprocal(out=stat[:, 6:7], in_=stat[:, 5:6])
                self.vflush()
                vr_s.inc(vec.tensor_tensor(out=stat[:, 6:7], in0=stat[:, 6:7], in1=stat[:, 0:1],
                                           op=ALU.mult))
                vr_s.wait(act)
                act.activation(out=stat[:, 7:8], in_=stat[:, 6:7], func=AF.Ln, bias=cst[:, 0:1])
                ar_s.inc(act.activation(out=stat[:, 7:8], in_=stat[:, 7:8], func=AF.Exp,
                                        scale=-0.5))
                ar_s.wait(vec)
                vec.tensor_copy(out=stat[:, 12:13], in_=stat[:, 1:2])
                vec.tensor_copy(out=stat[:, 13:14], in_=stat[:, 3:4])
                vr_s.inc(vec.tensor_tensor(out=stat[:, 14:15], in0=stat[:, 3:4],
                                           in1=stat[:, 7:8], op=ALU.mult))
                vr_s.wait(syn)
                dsp.inc(syn.dma_start(out=self.std_[tt * 128:(tt + 1) * 128, 0:3],
                                      in_=stat[:, 12:15]), 16)
                vr_land.append(vr_s.n)
                ar_land.append(ar_s.n)
            self.s1b_dsp = dsp.n

    def p2a(self):
        self.fence()
        nc = self.nc
        syn, act, vec, ten, gps = self.syn, self.act, self.vec, self.ten, self.gps
        stat, cst, wln, kfm, vtm = self.stat, self.cst, self.wln, self.k_fm, self.v_tm
        d2x = self.sem("d2x")
        d2q = self.sem("d2q")
        dyn = self.sem("dyn")
        p2p = self.sem("p2p")
        v2p = self.sem("v2p")
        v2m = self.sem("v2m")
        a2p = self.sem("a2p")
        dgw2 = self.sem("dgw2")

        _st = ExitStack()
        wqm = _st.enter_context(nc.sbuf_tensor("wqm", [128, 8 * DM], BF16))
        wkva_sb = _st.enter_context(nc.sbuf_tensor("wkva_sb", [128, 8 * 160], BF16))
        wkvb_sb = _st.enter_context(nc.sbuf_tensor("wkvb_sb", [128, 512], BF16))
        x1t = _st.enter_context(nc.sbuf_tensor("x1t", [128, 2 * DM], F32))
        stt = _st.enter_context(nc.sbuf_tensor("stt", [128, 2 * 4], F32))
        yntm = _st.enter_context(nc.sbuf_tensor("yntm", [128, 2 * DM], BF16))
        ynfm = _st.enter_context(nc.sbuf_tensor("ynfm", [128, 2 * 1024], BF16, is_transpose_dest=True))
        qtm = _st.enter_context(nc.sbuf_tensor("qtm", [128, 2 * DM], BF16))
        qsc = _st.enter_context(nc.sbuf_tensor("qsc", [128, 2 * 256], BF16))
        mcq = _st.enter_context(nc.sbuf_tensor("mcq", [128, 2 * 256], BF16))
        msq = _st.enter_context(nc.sbuf_tensor("msq", [128, 2 * 256], BF16))
        latfm = _st.enter_context(nc.sbuf_tensor("latfm", [128, 128], BF16))
        krf = _st.enter_context(nc.sbuf_tensor("krf", [32, 128], BF16))
        krtm = _st.enter_context(nc.sbuf_tensor("krtm", [128, 32], BF16))
        krt2 = _st.enter_context(nc.sbuf_tensor("krt2", [128, 32], BF16))
        krsc = _st.enter_context(nc.sbuf_tensor("krsc", [128, 64], BF16))
        sqn = _st.enter_context(nc.sbuf_tensor("sqn", [128, DM], BF16))
        gr16 = _st.enter_context(nc.sbuf_tensor("gr16", [128, 16], F32))
        r2c = _st.enter_context(nc.sbuf_tensor("r2c", [128, 1], F32))
        pq2 = _st.enter_context(nc.psum_tensor("pq2", [128, 2 * SC], F32))
        plat = _st.enter_context(nc.psum_tensor("plat", [128, SC], F32))
        prp = _st.enter_context(nc.psum_tensor("prp", [128, SC], F32))
        pnp = _st.enter_context(nc.psum_tensor("pnp", [128, SC], F32))
        pvt = _st.enter_context(nc.psum_tensor("pvt", [128, SC], F32))
        ptrb = _st.enter_context(nc.psum_tensor("ptrb", [128, 128], BF16))
        if True:

            gps.memset(vtm.ap().rearrange("p (s h c) -> p s h c", h=MH, c=33)[:, :, :, 32:33],
                       1.0)
            self.gp.inc(self.gflush())
            for ci in range(8):
                dgw2.inc(gps.dma_start(out=wqm[:, ci * DM:(ci + 1) * DM],
                                       in_=self.wqmlad[ci * 128:(ci + 1) * 128, :]), 16)
            for ci in range(8):
                dgw2.inc(gps.dma_start(out=wkva_sb[:, ci * 160:(ci + 1) * 160],
                                       in_=self.wkvad[ci * 128:(ci + 1) * 128, :]), 16)
            dgw2.inc(gps.dma_start(out=wkvb_sb[:], in_=self.wkvbd), 16)

            vland, aland, pland = [], [], []
            for ts in range(32):
                bs = ts % 2
                t0 = ts * 128
                if ts >= 2:
                    v2p.wait(syn, vland[ts - 2])
                    a2p.wait(syn, aland[ts - 2])
                d2x.inc(syn.dma_start(out=x1t[:, bs * DM:(bs + 1) * DM],
                                      in_=self.x1d[t0:t0 + 128, :]), 16)
                d2x.inc(syn.dma_start(out=stt[:, bs * 4:bs * 4 + 3],
                                      in_=self.std_[t0:t0 + 128, 0:3]), 16)
                d2x.inc(syn.dma_start(out=mcq[:, bs * 256:(bs + 1) * 256],
                                      in_=self.mcosqd[:, ts * 256:(ts + 1) * 256]), 16)
                d2x.inc(syn.dma_start(out=msq[:, bs * 256:(bs + 1) * 256],
                                      in_=self.msinqd[:, ts * 256:(ts + 1) * 256]), 16)
                d2x.wait(vec)
                if ts == 0:
                    self.ds.wait(vec)
                vec.tensor_scalar(out=yntm[:, bs * DM:(bs + 1) * DM],
                                  in0=x1t[:, bs * DM:(bs + 1) * DM],
                                  scalar1=stt[:, bs * 4:bs * 4 + 1],
                                  scalar2=stt[:, bs * 4 + 2:bs * 4 + 3],
                                  op0=ALU.subtract, op1=ALU.mult)
                v2m.inc(self.vflush())
                if _DUMP and ts < 2:
                    v2m.wait(syn)
                    self.dmp.inc(syn.dma_start(out=self.dbg5d[:, ts * 1024:(ts + 1) * 1024],
                                               in_=yntm[:, bs * DM:(bs + 1) * DM]), 16)
                v2m.wait(syn)
                if ts >= 2:
                    p2p.wait(syn, pland[ts - 2])
                for ci in range(8):
                    dyn.inc(syn.dma_start_transpose(
                        out=ynfm[:, bs * 1024 + ci * 128:bs * 1024 + (ci + 1) * 128],
                        in_=yntm[:, bs * DM + ci * 128:bs * DM + (ci + 1) * 128]), 16)
                dyn.wait(vec)
                for ci in range(8):
                    vec.tensor_scalar(out=ynfm[:, bs * 1024 + ci * 128:bs * 1024 + (ci + 1) * 128],
                                      in0=ynfm[:, bs * 1024 + ci * 128:bs * 1024 + (ci + 1) * 128],
                                      scalar1=wln[:, ci:ci + 1], scalar2=None, op0=ALU.mult)
                v2m.inc(self.vflush())
                if _DUMP and ts < 2:
                    v2m.wait(syn)
                    self.dmp.inc(syn.dma_start(
                        out=self.dbg5d[:, 2048 + ts * 1024:2048 + (ts + 1) * 1024],
                        in_=ynfm[:, bs * 1024:(bs + 1) * 1024]), 16)
                v2m.wait(ten)
                if ts == 0:
                    dgw2.wait(ten)
                if ts >= 1:
                    v2p.wait(ten, vland[ts - 1])
                    a2p.wait(ten, aland[ts - 1])
                yn = lambda ci: ynfm[:, bs * 1024 + ci * 128:bs * 1024 + (ci + 1) * 128]
                for fh in range(2):
                    for ci in range(8):
                        ten.matmul(pq2[:, fh * SC:(fh + 1) * SC], yn(ci),
                                   wqm[:, ci * DM + fh * SC:ci * DM + (fh + 1) * SC],
                                   start=(ci == 0), stop=(ci == 7))
                for ci in range(8):
                    ten.matmul(plat[:, 0:128],
                               wkva_sb[:, ci * 160:ci * 160 + 128], yn(ci),
                               start=(ci == 0), stop=(ci == 7))
                for ci in range(8):
                    mm_ = ten.matmul(prp[0:32, 0:128],
                                     wkva_sb[:, ci * 160 + 128:ci * 160 + 160], yn(ci),
                                     start=(ci == 0), stop=(ci == 7))
                p2p.inc(mm_)
                p2p.wait(vec)
                if ts >= 2:
                    d2q.wait(vec, 16 * (ts - 1))
                vec.tensor_copy(out=qtm[:, bs * DM:bs * DM + SC], in_=pq2[:, 0:SC])
                vec.tensor_copy(out=qtm[:, bs * DM + SC:(bs + 1) * DM], in_=pq2[:, SC:2 * SC])
                vec.tensor_copy(out=latfm[:, :], in_=plat[:, 0:128])
                vec.tensor_copy(out=krf[0:32, :], in_=prp[0:32, 0:128])
                v2m.inc(self.vflush())
                v2m.wait(ten)
                ten.matmul(pvt[:, 0:SC], latfm[:, :], wkvb_sb[:, :], start=True, stop=True)
                for hh in range(4):
                    ten.matmul(pnp[0:32, hh * 128:(hh + 1) * 128],
                               wkvb_sb[:, (hh) * 32:(hh + 1) * 32], latfm[:, :],
                               start=True, stop=True)
                mm_ = ten.transpose(ptrb[0:128, 0:32], krf[0:32, :], self.idb[0:32, 0:32])
                p2p.inc(mm_)
                p2p.wait(vec)
                vec.tensor_copy(out=krtm[:, :], in_=ptrb[0:128, 0:32])
                v2m.inc(self.vflush())
                v2m.wait(act)
                act.activation(out=sqn[:, 0:32], in_=krtm[:, :], func=AF.Square,
                               accum_out=r2c[:, 0:1])
                a2p.inc(self.aflush())
                kc = self.mck[:, ts * 16:(ts + 1) * 16]
                ks = self.msk[:, ts * 16:(ts + 1) * 16]
                x0v, x1v = krtm[:, 0:16], krtm[:, 16:32]
                a2p.wait(vec)
                vec.tensor_tensor(out=krsc[:, 0:16], in0=x0v, in1=kc, op=ALU.mult)
                vec.tensor_tensor(out=krsc[:, 16:32], in0=x1v, in1=ks, op=ALU.mult)
                vec.tensor_tensor(out=krsc[:, 32:48], in0=x0v, in1=ks, op=ALU.mult)
                vec.tensor_tensor(out=krsc[:, 48:64], in0=x1v, in1=kc, op=ALU.mult)
                vec.tensor_tensor(out=krt2[:, 0:16], in0=krsc[:, 0:16], in1=krsc[:, 16:32],
                                  op=ALU.subtract)
                vec.tensor_tensor(out=krt2[:, 16:32], in0=krsc[:, 32:48], in1=krsc[:, 48:64],
                                  op=ALU.add)
                v2m.inc(self.vflush())
                v2m.wait(ten)
                mm_ = ten.transpose(ptrb[0:32, 0:128], krt2[:, :], self.idb[:, :])
                p2p.inc(mm_)
                p2p.wait(vec)
                for pr_ in range(8):
                    col = pr_ * T + t0
                    vec.tensor_copy(out=kfm[32:64, col:col + 128], in_=ptrb[0:32, 0:128])
                    vec.tensor_copy(out=kfm[96:128, col:col + 128], in_=ptrb[0:32, 0:128])
                for hb in range(4):
                    if hb > 0:
                        v2m.wait(ten)
                    for hh in range(4):
                        h = hb * 4 + hh
                        pr_, rb = h // 2, (h % 2) * 64
                        vec.tensor_copy(out=kfm[rb:rb + 32, pr_ * T + t0:pr_ * T + t0 + 128],
                                        in_=pnp[0:32, hh * 128:(hh + 1) * 128])
                    if hb < 3:
                        v2m.inc(self.vflush())
                        v2m.wait(ten)
                        for hh in range(4):
                            h = (hb + 1) * 4 + hh
                            mm_ = ten.matmul(pnp[0:32, hh * 128:(hh + 1) * 128],
                                             wkvb_sb[:, h * 32:(h + 1) * 32], latfm[:, :],
                                             start=True, stop=True)
                        p2p.inc(mm_)
                        p2p.wait(vec)
                if ts == 0:
                    self.gp.wait(vec)
                vec.tensor_copy(
                    out=vtm.ap().rearrange("p (s h c) -> p s h c", h=MH, c=33)[:, ts, :, 0:32],
                    in_=pvt.ap().rearrange("p (h c) -> p h c", c=32)[:, 0:MH, :])
                v2m.inc(self.vflush())
                qq = qtm[:, bs * DM:(bs + 1) * DM].rearrange("p (h d) -> p h d", d=HD)
                q0 = qq[:, :, NOPE:NOPE + 16]
                q1 = qq[:, :, NOPE + 16:NOPE + 32]
                c3 = mcq[:, bs * 256:(bs + 1) * 256].rearrange("p (h f) -> p h f", f=16)
                s3 = msq[:, bs * 256:(bs + 1) * 256].rearrange("p (h f) -> p h f", f=16)
                qs0 = qsc[:, 0:256].rearrange("p (h f) -> p h f", f=16)
                qs1 = qsc[:, 256:512].rearrange("p (h f) -> p h f", f=16)
                vec.tensor_tensor(out=qs0, in0=q0, in1=c3, op=ALU.mult)
                vec.tensor_tensor(out=qs1, in0=q1, in1=s3, op=ALU.mult)
                vec.tensor_tensor(out=qs1, in0=qs0, in1=qs1, op=ALU.subtract)
                vec.tensor_tensor(out=qs0, in0=q0, in1=s3, op=ALU.mult)
                vec.tensor_tensor(out=q1, in0=q1, in1=c3, op=ALU.mult)
                vec.tensor_tensor(out=q1, in0=q1, in1=qs0, op=ALU.add)
                vec.tensor_copy(out=q0, in_=qs1)
                v2m.inc(self.vflush())
                v2m.wait(syn)
                d2q.inc(syn.dma_start(out=self.qd[t0:t0 + 128, :],
                                      in_=qtm[:, bs * DM:(bs + 1) * DM]), 16)
                v2m.wait(act)
                act.activation(out=sqn[:, :], in_=qtm[:, bs * DM:(bs + 1) * DM], func=AF.Square)
                a2p.inc(self.aflush())
                a2p.wait(vec)
                vec.reduce_sum(out=gr16[:, :],
                               in_=sqn.ap().rearrange("p (h d) -> p h d", d=HD), axis=AX.X)
                self.vflush()
                vec.tensor_tensor(out=self.nmax[:, 0:16], in0=self.nmax[:, 0:16],
                                  in1=gr16[:, :], op=ALU.max)
                v2m.inc(self.vflush())
                v2m.wait(act)
                act.activation(out=sqn[:, 0:512], in_=pvt[:, 0:SC], func=AF.Square)
                a2p.inc(self.aflush())
                a2p.wait(vec)
                vec.reduce_sum(out=gr16[:, :],
                               in_=sqn[:, 0:512].rearrange("p (h d) -> p h d", d=32), axis=AX.X)
                self.vflush()
                vec.tensor_scalar(out=gr16[:, :], in0=gr16[:, :], scalar1=r2c[:, 0:1],
                                  scalar2=None, op0=ALU.add)
                self.vflush()
                vec.tensor_tensor(out=self.nmax[:, 16:32], in0=self.nmax[:, 16:32],
                                  in1=gr16[:, :], op=ALU.max)
                v2p.inc(self.vflush())
                vland.append(v2p.n)
                aland.append(a2p.n)
                pland.append(p2p.n)

            v2p.wait(ten)
            a2p.wait(ten)
            ten.transpose(pq2[0:16, 0:128], self.nmax[:, 0:16], self.idf[:, :])
            mm_ = ten.transpose(pq2[0:16, SC:SC + 128], self.nmax[:, 16:32], self.idf[:, :])
            p2p.inc(mm_)
            p2p.wait(vec)
            vec.reduce_max(out=self.nmax[0:16, 32:33], in_=pq2[0:16, 0:128], axis=AX.X)
            vec.reduce_max(out=self.nmax[0:16, 33:34], in_=pq2[0:16, SC:SC + 128], axis=AX.X)
            self.vflush()
            vec.tensor_tensor(out=self.nmax[0:16, 34:35], in0=self.nmax[0:16, 32:33],
                              in1=self.nmax[0:16, 33:34], op=ALU.mult)
            v2p.inc(self.vflush())
            v2p.wait(act)
            act.activation(out=self.nmax[0:16, 34:35], in_=self.nmax[0:16, 34:35], func=AF.Sqrt)
            a2p.inc(self.aflush())
            a2p.wait(vec)
            vec.tensor_scalar(out=self.nmax[0:16, 35:36], in0=self.nmax[0:16, 34:35],
                              scalar1=-0.1275, scalar2=None, op0=ALU.mult)
            v2p.inc(self.vflush())
            v2p.wait(ten)
            mm_ = ten.transpose(prp[0:1, 0:16], self.nmax[0:16, 35:36], self.idf[0:16, 0:16])
            p2p.inc(mm_)
            p2p.wait(vec)
            vec.tensor_copy(out=gr16[0:1, :], in_=prp[0:1, 0:16])
            v2p.inc(self.vflush())
            v2p.wait(ten)
            ten.matmul(pnp[:, 0:16], self.ones_f[0:1, :], gr16[0:1, 0:16], start=True, stop=True)
            mm_ = ten.matmul(pnp[:, 16:24], self.ones_f[0:1, :], self.b8row[0:1, :],
                             start=True, stop=True)
            p2p.inc(mm_)
            p2p.wait(vec)
            vec.tensor_copy(out=self.bias_all[:, :], in_=pnp[:, 0:16])
            vec.tensor_copy(out=self.biasg[:, :], in_=pnp[:, 16:24])
            v2p.inc(self.vflush())
            self.p2a_tot = dict(v2p=v2p.n, a2p=a2p.n, p2p=p2p.n, d2q=d2q.n)
        _st.close()

    def p2b(self):
        nc = self.nc
        syn, act, vec, ten, gps = self.syn, self.act, self.vec, self.ten, self.gps
        b_qk, b_ex, b_pv, b_tri = (self.sem("b_qk"), self.sem("b_ex"), self.sem("b_pv"),
                                   self.sem("b_tri"))
        b_rc, b_bc, b_fin, b_wo, b_ep = (self.sem("b_rc"), self.sem("b_bc"), self.sem("b_fin"),
                                         self.sem("b_wo"), self.sem("b_ep"))
        b_dq, b_dx, b_dy = self.sem("b_dq"), self.sem("b_dx"), self.sem("b_dy")
        dgw3 = self.sem("dgw3")
        pa = self.p2a_tot
        self.fence()
        if _DUMP:
            for pr in range(8):
                self.dmp.inc(syn.dma_start(out=self.dbg2d[pr * 128:(pr + 1) * 128, :],
                                           in_=self.k_fm[:, pr * T:(pr + 1) * T]), 16)
            self.dmp.inc(syn.dma_start(out=self.dbg3d, in_=self.v_tm[:, :]), 16)
            self.dmp.inc(syn.dma_start(out=self.dbg4d[:, 0:16], in_=self.bias_all[:, :]), 16)
            self.dmp.inc(syn.dma_start(out=self.dbg4d[:, 16:24], in_=self.biasg[:, :]), 16)
            self.dmp.inc(syn.dma_start(out=self.dbg4d[:, 24:60], in_=self.nmax[:, :]), 16)

        _st = ExitStack()
        woef = _st.enter_context(nc.sbuf_tensor("woef", [128, 4 * DM], BF16))
        qfo = _st.enter_context(nc.sbuf_tensor("qfo", [128, 8 * 2048], BF16, is_transpose_dest=True))
        pt0 = _st.enter_context(nc.sbuf_tensor("pt0", [128, 1024], BF16))
        pt1 = _st.enter_context(nc.sbuf_tensor("pt1", [128, 1024], BF16))
        oTe = _st.enter_context(nc.sbuf_tensor("oTe", [128, 4 * 1024], BF16))
        rdrow = _st.enter_context(nc.sbuf_tensor("rdrow", [1, 1024], F32))
        rbbc = _st.enter_context(nc.sbuf_tensor("rbbc", [32, 1024], F32))
        x1t2 = _st.enter_context(nc.sbuf_tensor("x1t2", [128, 2 * DM], F32))
        stt2 = _st.enter_context(nc.sbuf_tensor("stt2", [128, 2 * 4], F32))
        mlasb = _st.enter_context(nc.sbuf_tensor("mlasb", [128, 2 * DM], F32))
        y1sb = _st.enter_context(nc.sbuf_tensor("y1sb", [128, 2 * DM], F32))
        ps0 = _st.enter_context(nc.psum_tensor("ps0", [128, 1024], F32))
        ps1 = _st.enter_context(nc.psum_tensor("ps1", [128, 1024], F32))
        poT = _st.enter_context(nc.psum_tensor("poT", [128, 1024], F32))
        pbc = _st.enter_context(nc.psum_tensor("pbc", [128, 1024], F32))
        if True:

            for vc in range(4):
                dgw3.inc(gps.dma_start(out=woef[:, vc * DM:(vc + 1) * DM],
                                       in_=self.woeffd[vc * 128:(vc + 1) * 128, :]), 16)
            pts = [pt0, pt1]
            pss = [ps0, ps1]

            def emit(which, eng, role):
                qk = ex = pv = tri_n = fin = wo = ep = dx = 0
                eng.wait_ge(self._sems["v2p"].h, pa["v2p"])
                eng.wait_ge(self._sems["a2p"].h, pa["a2p"])
                eng.wait_ge(self._sems["p2p"].h, pa["p2p"])
                if which == "ten":
                    eng.wait_ge(dgw3.h, 64)
                for qc in range(2):
                    qb = QB[role][qc]
                    E = (qb + 1024) // 128
                    if which == "act":
                        if qc == 0:
                            eng.wait_ge(self._sems["d2q"].h, pa["d2q"])
                        for pr_ in range(8):
                            for tt in range(8):
                                eng.dma_start_transpose(
                                    out=qfo[:, pr_ * 2048 + qc * 1024 + tt * 128:
                                            pr_ * 2048 + qc * 1024 + (tt + 1) * 128],
                                    in_=self.qd[qb + tt * 128:qb + (tt + 1) * 128,
                                                pr_ * 128:(pr_ + 1) * 128]
                                ).then_inc(b_dq.h, 16)
                    for h in range(MH):
                        hh = 16 * qc + h
                        pr_, rb = h // 2, (h % 2) * 64
                        kts = list(range(E - 1, -1, -1))
                        first_b0 = next(kt for kt in kts if max(kt * 128 - qb, 0) < 512)
                        descs = []
                        tn_run = tri_n
                        for i, kt in enumerate(kts):
                            r = kt * 128 - qb
                            diag = (r >= 0)
                            if diag:
                                tn_run += 1
                            descs.append(dict(i=i, kt=kt, r=r, lo=max(r, 0), diag=diag,
                                              g=qk + i, tri_after=tn_run))

                        def emit_qk(d):
                            if d["g"] >= 2:
                                eng.wait_ge(b_ex.h, d["g"] - 1)
                            if qc == 1 and h == 0 and d["i"] < 2:
                                eng.wait_ge(b_ep.h, 8)
                            if d["i"] == 0:
                                eng.wait_ge(b_fin.h, hh)
                                eng.wait_ge(b_dq.h, 16 * (qc * 64 + (pr_ + 1) * 8))
                            kt = d["kt"]
                            for nn in range(2):
                                mm_ = eng.matmul(pss[d["g"] % 2][:, nn * 512:(nn + 1) * 512],
                                                 self.k_fm[rb:rb + 64,
                                                           pr_ * T + kt * 128:pr_ * T + (kt + 1) * 128],
                                                 qfo[rb:rb + 64,
                                                     pr_ * 2048 + qc * 1024 + nn * 512:
                                                     pr_ * 2048 + qc * 1024 + (nn + 1) * 512],
                                                 start=True, stop=True)
                            mm_.then_inc(b_qk.h, 1)

                        def emit_pv(d):
                            eng.wait_ge(b_ex.h, d["g"] + 1)
                            if d["diag"]:
                                eng.wait_ge(b_tri.h, d["tri_after"])
                            kt, lo = d["kt"], d["lo"]
                            segs = []
                            if lo < 512:
                                segs.append((lo, 512, 0))
                            segs.append((max(lo, 512), 1024, 1))
                            for (a, bnd, bank) in segs:
                                fw = (bank == 1 and d["i"] == 0) or (bank == 0 and kt == first_b0)
                                mm_ = eng.matmul(poT[0:33, a:bnd],
                                                 self.v_tm[:, kt * 528 + h * 33:
                                                           kt * 528 + (h + 1) * 33],
                                                 pts[d["g"] % 2][:, a:bnd],
                                                 start=fw, stop=(kt == 0))
                            mm_.then_inc(b_pv.h, 1)

                        for i, d in enumerate(descs):
                            g = d["g"]
                            lo = d["lo"]
                            if which == "ten":
                                emit_qk(d)
                                if i >= 1:
                                    emit_pv(descs[i - 1])
                                if i == len(descs) - 1:
                                    emit_pv(d)
                            if which == "act":
                                eng.wait_ge(b_qk.h, g + 1)
                                if g >= 2:
                                    eng.wait_ge(b_pv.h, g - 1)
                                eng.activation(out=pts[g % 2][:, lo:1024],
                                               in_=pss[g % 2][:, lo:1024], func=AF.Exp,
                                               scale=0.125, bias=self.bias_all[:, h:h + 1]
                                               ).then_inc(b_ex.h, 1)
                            if which == "vec" and d["diag"]:
                                eng.wait_ge(b_ex.h, g + 1)
                                eng.tensor_tensor(out=pts[g % 2][:, d["r"]:d["r"] + 128],
                                                  in0=pts[g % 2][:, d["r"]:d["r"] + 128],
                                                  in1=self.tri[:, :], op=ALU.mult
                                                  ).then_inc(b_tri.h, 1)
                            qk += 1
                            ex += 1
                            pv += 1
                            if d["diag"]:
                                tri_n += 1
                        if which == "vec":
                            eng.wait_ge(b_pv.h, pv)
                            if qc == 1 and h == 0:
                                eng.wait_ge(b_wo.h, 8)
                            eng.reciprocal(out=rdrow[0:1, :], in_=poT[32:33, :]
                                           ).then_inc(b_rc.h, 1)
                        if which == "ten":
                            eng.wait_ge(b_rc.h, hh + 1)
                            if hh >= 1:
                                eng.wait_ge(b_fin.h, hh)
                            for nn in range(2):
                                mm_ = eng.matmul(pbc[0:32, nn * 512:(nn + 1) * 512],
                                                 self.ones_f[0:1, 0:32],
                                                 rdrow[0:1, nn * 512:(nn + 1) * 512],
                                                 start=True, stop=True)
                            mm_.then_inc(b_bc.h, 1)
                        if which == "vec":
                            eng.wait_ge(b_bc.h, hh + 1)
                            eng.tensor_copy(out=rbbc[0:32, :], in_=pbc[0:32, :])
                            eng.tensor_tensor(out=oTe[(h % 4) * 32:(h % 4) * 32 + 32,
                                                      (h // 4) * 1024:(h // 4 + 1) * 1024],
                                              in0=poT[0:32, :], in1=rbbc[0:32, :],
                                              op=ALU.mult).then_inc(b_fin.h, 1)
                        fin += 1
                    for tsub in range(8):
                        eidx = 8 * qc + tsub
                        bs = eidx % 2
                        can0 = qc * 1024 + tsub * 128
                        tok = qb + tsub * 128
                        if which == "syn":
                            if eidx >= 2:
                                eng.wait_ge(b_ep.h, eidx - 1)
                            eng.dma_start(out=x1t2[:, bs * DM:(bs + 1) * DM],
                                          in_=self.x1d[tok:tok + 128, :]).then_inc(b_dx.h, 16)
                            eng.dma_start(out=stt2[:, bs * 4:bs * 4 + 3],
                                          in_=self.std_[tok:tok + 128, 0:3]).then_inc(b_dx.h, 16)
                            eng.wait_ge(b_dx.h, dx + 32)
                            eng.dma_start(out=self.x1cd[can0:can0 + 128, :],
                                          in_=x1t2[:, bs * DM:(bs + 1) * DM]).then_inc(b_dy.h, 16)
                            eng.wait_ge(b_ep.h, eidx + 1)
                            eng.dma_start(out=self.y1d[can0:can0 + 128, :],
                                          in_=y1sb[:, bs * DM:(bs + 1) * DM]).then_inc(b_dy.h, 16)
                        if which == "ten":
                            eng.wait_ge(b_fin.h, 16 * (qc + 1))
                            eng.wait_ge(b_ex.h, ex)
                            if eidx >= 2:
                                eng.wait_ge(b_ep.h, eidx - 1)
                            for fh in range(2):
                                for vc in range(4):
                                    mm_ = eng.matmul(pss[eidx % 2][:, fh * 512:(fh + 1) * 512],
                                                     oTe[:, vc * 1024 + tsub * 128:
                                                         vc * 1024 + (tsub + 1) * 128],
                                                     woef[:, vc * DM + fh * 512:
                                                          vc * DM + (fh + 1) * 512],
                                                     start=(vc == 0), stop=(vc == 3))
                            mm_.then_inc(b_wo.h, 1)
                        if which == "vec":
                            eng.wait_ge(b_wo.h, eidx + 1)
                            eng.wait_ge(b_dx.h, dx + 32)
                            eng.tensor_copy(out=mlasb[:, bs * DM:bs * DM + 512],
                                            in_=pss[eidx % 2][:, 0:512])
                            eng.tensor_copy(out=mlasb[:, bs * DM + 512:(bs + 1) * DM],
                                            in_=pss[eidx % 2][:, 512:1024])
                            eng.tensor_scalar(out=y1sb[:, bs * DM:(bs + 1) * DM],
                                              in0=x1t2[:, bs * DM:(bs + 1) * DM],
                                              scalar1=stt2[:, bs * 4:bs * 4 + 1],
                                              scalar2=stt2[:, bs * 4 + 1:bs * 4 + 2],
                                              op0=ALU.subtract, op1=ALU.mult)
                            eng.tensor_tensor(out=y1sb[:, bs * DM:(bs + 1) * DM],
                                              in0=y1sb[:, bs * DM:(bs + 1) * DM],
                                              in1=mlasb[:, bs * DM:(bs + 1) * DM],
                                              op=ALU.add).then_inc(b_ep.h, 1)
                        wo += 1
                        ep += 1
                        dx += 32
                return dict(qk=qk, ex=ex, pv=pv, tri=tri_n, fin=fin, wo=wo, ep=ep, dx=dx)

            tot = {}
            if _RFIX is not None:
                tot["t0"] = tot["t1"] = emit("ten", ten, _RFIX)
                tot["a0"] = tot["a1"] = emit("act", act, _RFIX)
                tot["v0"] = tot["v1"] = emit("vec", vec, _RFIX)
                tot["s0"] = tot["s1"] = emit("syn", syn, _RFIX)
            if _RFIX is None:
              with nc.Block() as block:
                  @block.tensor
                  def _(eng):
                      with eng.register("role_t") as r:
                          self.ds.wait(eng)
                          eng.reg_load(r, self.srole[0:1, 0:1])
                          with eng.If_eq(r, 0):
                              tot["t0"] = emit("ten", eng, 0)
                          with eng.If_eq(r, 1):
                              tot["t1"] = emit("ten", eng, 1)

                  @block.scalar
                  def _(eng):
                      with eng.register("role_a") as r:
                          self.ds.wait(eng)
                          eng.reg_load(r, self.srole[0:1, 0:1])
                          with eng.If_eq(r, 0):
                              tot["a0"] = emit("act", eng, 0)
                          with eng.If_eq(r, 1):
                              tot["a1"] = emit("act", eng, 1)

                  @block.vector
                  def _(eng):
                      with eng.register("role_v") as r:
                          self.ds.wait(eng)
                          eng.reg_load(r, self.srole[0:1, 0:1])
                          with eng.If_eq(r, 0):
                              tot["v0"] = emit("vec", eng, 0)
                          with eng.If_eq(r, 1):
                              tot["v1"] = emit("vec", eng, 1)

                  @block.sync
                  def _(eng):
                      with eng.register("role_s") as r:
                          self.ds.wait(eng)
                          eng.reg_load(r, self.srole[0:1, 0:1])
                          with eng.If_eq(r, 0):
                              tot["s0"] = emit("syn", eng, 0)
                          with eng.If_eq(r, 1):
                              tot["s1"] = emit("syn", eng, 1)

            assert tot["t0"] == tot["t1"] == tot["a0"] == tot["a1"] == tot["v0"] == tot["v1"] \
                == tot["s0"] == tot["s1"], tot
            t = tot["t0"]
            b_qk.n = b_ex.n = b_pv.n = t["qk"]
            b_tri.n = t["tri"]
            b_rc.n = b_bc.n = b_fin.n = t["fin"]
            b_wo.n = b_ep.n = t["wo"]
            b_dq.n = 16 * 128
            b_dx.n = t["dx"]
            b_dy.n = 32 * t["ep"]
        _st.close()

    def p3(self):
        nc = self.nc
        syn, act, vec, ten, gps = self.syn, self.act, self.vec, self.ten, self.gps
        stat, cst, wln = self.stat, self.cst, self.wln
        BL, NBL = 512, 4
        d3 = self.sem("d3")
        d3w = self.sem("d3w")
        d3t = self.sem("d3t")
        dout = self.sem("dout")
        p3p, v3p, a3p = self.sem("p3p"), self.sem("v3p"), self.sem("a3p")
        p3d = self.sem("p3d")
        b_ep, b_dy = self._sems["b_ep"], self._sems["b_dy"]
        self.fence()

        _st = ExitStack()
        y1t = _st.enter_context(nc.sbuf_tensor("y1t", [128, 4 * DM], F32))
        yn2tm = _st.enter_context(nc.sbuf_tensor("yn2tm", [128, DM], BF16))
        yn2fm = _st.enter_context(nc.sbuf_tensor("yn2fm", [128, 8 * BL], BF16, is_transpose_dest=True))
        wgt = _st.enter_context(nc.sbuf_tensor("wgt", [128, 8 * 8], BF16))
        lg = _st.enter_context(nc.sbuf_tensor("lg", [128, 8], F32))
        ge2 = _st.enter_context(nc.sbuf_tensor("ge2", [128, 8], F32))
        cmb = _st.enter_context(nc.sbuf_tensor("cmb", [128, 8], F32))
        cfm = _st.enter_context(nc.sbuf_tensor("cfm", [1, 8 * BL], F32))
        cbc = _st.enter_context(nc.sbuf_tensor("cbc", [128, 8 * BL], BF16))
        wgu_sb = _st.enter_context(nc.sbuf_tensor("wgu_sb", [128, 2 * 8 * DM], BF16))
        wd_sb = _st.enter_context(nc.sbuf_tensor("wd_sb", [128, 2 * 4 * DM], BF16))
        gsl = _st.enter_context(nc.sbuf_tensor("gsl", [128, 4 * 512], BF16))
        ubf = _st.enter_context(nc.sbuf_tensor("ubf", [128, 4 * 512], BF16))
        gup = _st.enter_context(nc.sbuf_tensor("gup", [128, 9 * 4 * BL], BF16))
        y2s = _st.enter_context(nc.sbuf_tensor("y2s", [128, 2 * DM], F32))
        x1c2 = _st.enter_context(nc.sbuf_tensor("x1c2", [128, 4 * DM], F32))
        osb = _st.enter_context(nc.sbuf_tensor("osb", [128, 4 * DM], F32))
        sq3 = _st.enter_context(nc.sbuf_tensor("sq3", [128, DM], BF16))
        pga = _st.enter_context(nc.psum_tensor("pga", [128, 2048], F32))
        pgb = _st.enter_context(nc.psum_tensor("pgb", [128, 2048], F32))
        if True:

            for ci in range(8):
                d3w.inc(gps.dma_start(out=wgt[:, ci * 8:(ci + 1) * 8],
                                      in_=self.wgated[ci * 128:(ci + 1) * 128, :]), 16)

            for bl in range(NBL):
                c0 = bl * BL
                b_dy.wait(syn)
                if bl >= 1:
                    v3p.wait(syn, v3p.n)
                    dout.wait(vec)
                for s in range(4):
                    d3.inc(syn.dma_start(out=y1t[:, s * DM:(s + 1) * DM],
                                         in_=self.y1d[c0 + s * 128:c0 + (s + 1) * 128, :]), 16)
                for s in range(4):
                    d3.wait(act)
                    if bl >= 1 and s == 0:
                        p3p.wait(act, p3p.n)
                    act.activation(out=sq3[:, :], in_=y1t[:, s * DM:(s + 1) * DM],
                                   func=AF.Square, accum_out=stat[:, 0:1])
                    act.activation(out=stat[:, 1:2], in_=stat[:, 0:1], func=AF.Ln,
                                   scale=1.0 / DM, bias=cst[:, 0:1])
                    a3p.inc(act.activation(out=stat[:, 1:2], in_=stat[:, 1:2], func=AF.Exp,
                                           scale=-0.5))
                    a3p.wait(vec)
                    v3p.inc(vec.tensor_scalar(out=yn2tm[:, :], in0=y1t[:, s * DM:(s + 1) * DM],
                                              scalar1=stat[:, 1:2], scalar2=None, op0=ALU.mult))
                    v3p.wait(act)
                    for ci in range(8):
                        d3t.inc(act.dma_start_transpose(
                            out=yn2fm[:, ci * BL + s * 128:ci * BL + (s + 1) * 128],
                            in_=yn2tm[:, ci * 128:(ci + 1) * 128]), 16)
                d3t.wait(vec)
                for ci in range(8):
                    vec.tensor_scalar(out=yn2fm[:, ci * BL:(ci + 1) * BL],
                                      in0=yn2fm[:, ci * BL:(ci + 1) * BL],
                                      scalar1=wln[:, 8 + ci:9 + ci], scalar2=None, op0=ALU.mult)
                v3p.inc(self.vflush())
                for s in range(4):
                    v3p.wait(ten)
                    if bl == 0 and s == 0:
                        d3w.wait(ten)
                    for ci in range(8):
                        ten.matmul(pga[:, 0:8],
                                   yn2fm[:, ci * BL + s * 128:ci * BL + (s + 1) * 128],
                                   wgt[:, ci * 8:(ci + 1) * 8],
                                   start=(ci == 0), stop=False)
                    mm_ = ten.matmul(pga[:, 0:8], self.ones_f[0:1, :],
                                     self.b8row[0:1, :], start=False, stop=True)
                    p3p.inc(mm_)
                    p3p.wait(vec)
                    vec.tensor_copy(out=lg[:, :], in_=pga[:, 0:8])
                    self.vflush()
                    vec.reduce_max(out=stat[:, 0:1], in_=lg[:, :], axis=AX.X)
                    self.vflush()
                    vec.tensor_scalar(out=cmb[:, :], in0=lg[:, :], scalar1=stat[:, 0:1],
                                      scalar2=1e9, op0=ALU.is_ge, op1=ALU.mult)
                    self.vflush()
                    vec.tensor_tensor(out=cmb[:, :], in0=lg[:, :], in1=cmb[:, :],
                                      op=ALU.subtract)
                    self.vflush()
                    vec.reduce_max(out=stat[:, 1:2], in_=cmb[:, :], axis=AX.X)
                    self.vflush()
                    vec.tensor_tensor(out=stat[:, 2:3], in0=stat[:, 0:1], in1=stat[:, 1:2],
                                      op=ALU.add)
                    self.vflush()
                    v3p.inc(vec.tensor_scalar(out=cmb[:, :], in0=lg[:, :], scalar1=2.0,
                                              scalar2=stat[:, 2:3], op0=ALU.mult,
                                              op1=ALU.subtract))
                    v3p.wait(act)
                    a3p.inc(act.activation(out=cmb[:, :], in_=cmb[:, :], func=AF.Sigmoid))
                    a3p.wait(vec)
                    vec.tensor_scalar(out=ge2[:, :], in0=lg[:, :], scalar1=stat[:, 1:2],
                                      scalar2=None, op0=ALU.is_ge)
                    self.vflush()
                    v3p.inc(vec.tensor_tensor(out=cmb[:, :], in0=cmb[:, :], in1=ge2[:, :],
                                              op=ALU.mult))
                    v3p.wait(ten)
                    for e in range(8):
                        mm_ = ten.transpose(pgb[0:1, e * 128:(e + 1) * 128], cmb[:, e:e + 1],
                                            self.idf[:, :])
                    p3p.inc(mm_)
                    p3p.wait(vec)
                    v3p.inc(vec.tensor_copy(
                        out=cfm.ap().rearrange("p (e c) -> p e c", c=BL)[:, :, s * 128:(s + 1) * 128],
                        in_=pgb[0:1, 0:1024].rearrange("p (e c) -> p e c", c=128)))
                for e in range(8):
                    v3p.wait(ten)
                    mm_ = ten.matmul(pga[:, 0:512], self.ones_f[0:1, :],
                                     cfm[0:1, e * BL:(e + 1) * BL], start=True, stop=True)
                    p3p.inc(mm_)
                    p3p.wait(vec)
                    v3p.inc(vec.tensor_copy(out=cbc[:, e * BL:(e + 1) * BL], in_=pga[:, 0:512]))
                guland = {}
                for e in range(9):
                    we = e % 2
                    if e >= 2:
                        p3p.wait(gps, guland[e - 2])
                    elif bl >= 1:
                        p3p.wait(gps, self._p3_guland_last)
                    for ci in range(8):
                        d3w.inc(gps.dma_start(
                            out=wgu_sb[:, we * 8 * DM + ci * DM:we * 8 * DM + (ci + 1) * DM],
                            in_=self.wgud[e * DM + ci * 128:e * DM + (ci + 1) * 128, :]), 16)
                    d3w.wait(ten)
                    if e == 0:
                        v3p.wait(ten)
                    if bl + e >= 1:
                        v3p.wait(ten, v3p.n)
                    for ic in range(4):
                        for ci in range(8):
                            ten.matmul(pga[:, ic * 512:(ic + 1) * 512],
                                       wgu_sb[:, we * 8 * DM + ci * DM + ic * 128:
                                              we * 8 * DM + ci * DM + (ic + 1) * 128],
                                       yn2fm[:, ci * BL:(ci + 1) * BL],
                                       start=(ci == 0), stop=(ci == 7))
                    for ic in range(4):
                        for ci in range(8):
                            mm_ = ten.matmul(pgb[:, ic * 512:(ic + 1) * 512],
                                             wgu_sb[:, we * 8 * DM + ci * DM + 512 + ic * 128:
                                                    we * 8 * DM + ci * DM + 512 + (ic + 1) * 128],
                                             yn2fm[:, ci * BL:(ci + 1) * BL],
                                             start=(ci == 0), stop=(ci == 7))
                    p3p.inc(mm_)
                    p3p.wait(act)
                    for ic in range(4):
                        act.activation(out=gsl[:, ic * 512:(ic + 1) * 512],
                                       in_=pga[:, ic * 512:(ic + 1) * 512], func=AF.Silu)
                    a3p.inc(self.aflush())
                    a3p.wait(vec)
                    p3p.wait(vec)
                    for ic in range(4):
                        vec.tensor_copy(out=ubf[:, ic * 512:(ic + 1) * 512],
                                        in_=pgb[:, ic * 512:(ic + 1) * 512])
                    for ic in range(4):
                        vec.tensor_tensor(out=gsl[:, ic * 512:(ic + 1) * 512],
                                          in0=gsl[:, ic * 512:(ic + 1) * 512],
                                          in1=ubf[:, ic * 512:(ic + 1) * 512], op=ALU.mult)
                    if e < 8:
                        for ic in range(4):
                            vec.tensor_tensor(out=gsl[:, ic * 512:(ic + 1) * 512],
                                              in0=gsl[:, ic * 512:(ic + 1) * 512],
                                              in1=cbc[:, e * BL:(e + 1) * BL], op=ALU.mult)
                    for ic in range(4):
                        vec.tensor_copy(out=gup[:, e * 4 * BL + ic * BL:e * 4 * BL + (ic + 1) * BL],
                                        in_=gsl[:, ic * 512:(ic + 1) * 512])
                    v3p.inc(self.vflush())
                    guland[e] = p3p.n
                self._p3_guland_last = p3p.n
                vland_dn = v3p.n
                for e in range(9):
                    we = e % 2
                    if p3d.n >= 2:
                        p3d.wait(gps, p3d.n - 1)
                    for ic in range(4):
                        d3w.inc(gps.dma_start(
                            out=wd_sb[:, we * 4 * DM + ic * DM:we * 4 * DM + (ic + 1) * DM],
                            in_=self.wdd[e * EI + ic * 128:e * EI + (ic + 1) * 128, :]), 16)
                    d3w.wait(ten)
                    if e == 0:
                        v3p.wait(ten, vland_dn)
                    for s in range(4):
                        for fh in range(2):
                            slot = s * 2 + fh
                            pd = pga if slot < 4 else pgb
                            so = (slot % 4) * 512
                            for ic in range(4):
                                mm_ = ten.matmul(pd[:, so:so + 512],
                                                 gup[:, e * 4 * BL + ic * BL + s * 128:
                                                     e * 4 * BL + ic * BL + (s + 1) * 128],
                                                 wd_sb[:, we * 4 * DM + ic * DM + fh * 512:
                                                      we * 4 * DM + ic * DM + (fh + 1) * 512],
                                                 start=(e == 0 and ic == 0), stop=(e == 8 and ic == 3))
                    p3d.inc(mm_)
                for s in range(4):
                    bs = s % 2
                    d3.inc(syn.dma_start(out=x1c2[:, s * DM:(s + 1) * DM],
                                         in_=self.x1cd[c0 + s * 128:c0 + (s + 1) * 128, :]), 16)
                    p3p.wait(vec)
                    p3d.wait(vec)
                    d3.wait(vec)
                    pd0 = pga if s * 2 < 4 else pgb
                    pd1 = pga if s * 2 + 1 < 4 else pgb
                    vec.tensor_copy(out=y2s[:, bs * DM:bs * DM + 512],
                                    in_=pd0[:, ((s * 2) % 4) * 512:((s * 2) % 4) * 512 + 512])
                    vec.tensor_copy(out=y2s[:, bs * DM + 512:(bs + 1) * DM],
                                    in_=pd1[:, ((s * 2 + 1) % 4) * 512:((s * 2 + 1) % 4) * 512 + 512])
                    v3p.inc(vec.tensor_tensor(out=y2s[:, bs * DM:(bs + 1) * DM],
                                              in0=y2s[:, bs * DM:(bs + 1) * DM],
                                              in1=y1t[:, s * DM:(s + 1) * DM], op=ALU.add))
                    v3p.wait(act)
                    act.activation(out=sq3[:, :], in_=y2s[:, bs * DM:(bs + 1) * DM],
                                   func=AF.Square, accum_out=stat[:, 0:1])
                    a3p.inc(act.activation(out=sq3[:, :], in_=y2s[:, bs * DM:(bs + 1) * DM],
                                           func=AF.Copy, accum_out=stat[:, 1:2]))
                    a3p.wait(vec)
                    vec.tensor_scalar(out=stat[:, 1:2], in0=stat[:, 1:2], scalar1=1.0 / DM,
                                      scalar2=None, op0=ALU.mult)
                    vec.tensor_scalar(out=stat[:, 0:1], in0=stat[:, 0:1], scalar1=1.0 / DM,
                                      scalar2=None, op0=ALU.mult)
                    vec.tensor_tensor(out=stat[:, 2:3], in0=stat[:, 1:2], in1=stat[:, 1:2],
                                      op=ALU.mult)
                    self.vflush()
                    v3p.inc(vec.tensor_tensor(out=stat[:, 0:1], in0=stat[:, 0:1],
                                              in1=stat[:, 2:3], op=ALU.subtract))
                    v3p.wait(act)
                    act.activation(out=stat[:, 2:3], in_=stat[:, 0:1], func=AF.Ln,
                                   bias=cst[:, 0:1])
                    a3p.inc(act.activation(out=stat[:, 2:3], in_=stat[:, 2:3], func=AF.Exp,
                                           scale=-0.5))
                    a3p.wait(vec)
                    vec.tensor_scalar(out=osb[:, s * DM:(s + 1) * DM],
                                      in0=y2s[:, bs * DM:(bs + 1) * DM],
                                      scalar1=stat[:, 1:2], scalar2=stat[:, 2:3],
                                      op0=ALU.subtract, op1=ALU.mult)
                    v3p.inc(vec.tensor_tensor(out=osb[:, s * DM:(s + 1) * DM],
                                              in0=osb[:, s * DM:(s + 1) * DM],
                                              in1=x1c2[:, s * DM:(s + 1) * DM], op=ALU.add))
                    v3p.wait(syn)
                    dout.inc(syn.dma_start(out=self.outd[c0 + s * 128:c0 + (s + 1) * 128, :],
                                           in_=osb[:, s * DM:(s + 1) * DM]), 16)
            dout.wait(syn)
        _st.close()


_NC_CACHE = {}


def _get_nc():
    if "nc" not in _NC_CACHE:
        _NC_CACHE["nc"] = Builder().build()
    return _NC_CACHE["nc"]


def kernel(**inputs):
    x = np.asarray(inputs["L0_attn_x"], np.float32)
    w = _stage_weights(inputs)
    c = _host_consts()
    common = {
        "wqkvg": w["wqkvg"], "wo_vf": w["wo_vf"], "wq_mla": w["wq_mla"], "wkva": w["wkva"],
        "wkvb": w["wkvb"], "woeff": w["woeff"], "wgate": w["wgate"], "bias8": w["bias8"],
        "wgu": w["wgu"], "wd": w["wd"], "wln1": w["wln1"], "wln2": w["wln2"], "gnw": w["gnw"],
        "cosfm": c["cosfm"], "sinfm": c["sinfm"], "maskt": c["maskt"], "cd": c["cd"],
        "kdec": c["kdec"], "mcosq": c["mcosq"], "msinq": c["msinq"], "mcosk": c["mcosk"],
        "msink": c["msink"], "tri": c["tri"], "idb": c["idb"], "idf": c["idf"],
    }
    in_maps = []
    for core in range(8):
        seq, role = core // 2, core % 2
        m = dict(common)
        m["x0"] = np.ascontiguousarray(x[seq])
        m["role"] = np.array([[role]], dtype=np.int32)
        in_maps.append(m)
    nc = _get_nc()
    res = run_bass_kernel_spmd(nc, in_maps, core_ids=list(range(8)))
    out = np.empty((4, T, DM), dtype=np.float32)
    for core in range(8):
        seq, role = core // 2, core % 2
        o = np.asarray(res.results[core]["out"])
        if role == 0:
            out[seq, 0:1024] = o[0:1024]
            out[seq, 3072:4096] = o[1024:2048]
        else:
            out[seq, 1024:3072] = o[0:2048]
    return out


if __name__ == "__main__":
    b = Builder()
    nc = b.build()
    n = sum(1 for _ in nc.all_instructions())
    print("build ok, instructions:", n)


# version 05 (final)
# speedup vs baseline: 1.0728x; 1.0728x over previous
import numpy as np
import ml_dtypes
from contextlib import ExitStack

import concourse.bass as bass
import concourse.mybir as mybir
from concourse.bass_utils import run_bass_kernel_spmd

F32 = mybir.dt.float32
BF16 = mybir.dt.bfloat16
I32 = mybir.dt.int32
AF = mybir.ActivationFunctionType
ALU = mybir.AluOpType
AX = mybir.AxisListType

T, DM = 4096, 1024
SC, NSC = 512, 8
RH, DK, DV = 4, 256, 512
MH, HD, NOPE, RP, RLAT = 16, 64, 32, 32, 128
NE, EI = 8, 512
EPS = 1e-5
QB = [[0, 3072], [1024, 2048]]
GAMMAS = [1.0 - 2.0 ** (-5.0 - h) for h in range(RH)]
_BF = ml_dtypes.bfloat16
_LIMIT = None
_S1A_H = None
_S1A_S = None
_STG = 99
_RFIX = None
_R1 = 2
_DUMP = 0


def _host_consts():
    t = np.arange(T, dtype=np.float64)
    invf = 1.0 / (10000.0 ** (np.arange(0, DK, 2, dtype=np.float64) / DK))
    ang = invf[:, None] * t[None, :]
    cosfm = np.cos(ang).astype(np.float32).astype(_BF)
    sinfm = np.sin(ang).astype(np.float32).astype(_BF)
    i512 = np.arange(SC, dtype=np.float64)
    maskT = np.zeros((RH * SC, SC), dtype=np.float32)
    cd = np.zeros((RH * 128, SC), dtype=np.float32)
    for h in range(RH):
        g = GAMMAS[h]
        diff = i512[None, :] - i512[:, None]
        maskT[h * SC:(h + 1) * SC, :] = np.where(diff >= 0, g ** np.maximum(diff, 0.0), 0.0) / 16.0
        cd[h * 128:(h + 1) * 128, :] = g ** (i512 + 1.0) / 16.0
    kdec = np.zeros((128, 16), dtype=np.float32)
    for sub in range(4):
        for h in range(RH):
            kdec[:, sub * 4 + h] = GAMMAS[h] ** (SC - 1.0 - (sub * 128 + np.arange(128)))
    mf = 1.0 / (100000.0 ** (np.arange(0, RP, 2, dtype=np.float64) / RP))
    mang = t[:, None] * mf[None, :]
    mcos = np.cos(mang).astype(np.float32).reshape(32, 128, 16)
    msin = np.sin(mang).astype(np.float32).reshape(32, 128, 16)
    mcosq = np.concatenate([np.tile(mcos[s], (1, MH)) for s in range(32)], axis=1).astype(_BF)
    msinq = np.concatenate([np.tile(msin[s], (1, MH)) for s in range(32)], axis=1).astype(_BF)
    mcosk = np.concatenate([mcos[s] for s in range(32)], axis=1).astype(_BF)
    msink = np.concatenate([msin[s] for s in range(32)], axis=1).astype(_BF)
    tri = (np.arange(128)[None, :] >= np.arange(128)[:, None]).astype(np.float32).astype(_BF)
    idb = np.eye(128, dtype=np.float32).astype(_BF)
    idf = np.eye(128, dtype=np.float32)
    return dict(cosfm=np.asarray(cosfm), sinfm=np.asarray(sinfm), maskt=maskT,
                cd=np.asarray(cd.astype(np.float32).astype(_BF)), kdec=kdec,
                mcosq=np.asarray(mcosq), msinq=np.asarray(msinq),
                mcosk=np.asarray(mcosk), msink=np.asarray(msink),
                tri=np.asarray(tri), idb=np.asarray(idb), idf=idf)


def _perm_rope_cols(w, base, d):
    w = np.array(w, dtype=np.float32, copy=True)
    idx = np.concatenate([np.arange(base, base + d, 2), np.arange(base + 1, base + d, 2)])
    w[:, base:base + d] = w[:, idx]
    return w


def _stage_weights(ins):
    wq = np.asarray(ins["L0_attn_Wq"], np.float32).T
    wk = np.asarray(ins["L0_attn_Wk"], np.float32).T
    wv = np.asarray(ins["L0_attn_Wv"], np.float32).T
    wg = np.asarray(ins["L0_attn_Wg"], np.float32).T
    blocks = []
    for h in range(RH):
        blocks += [wq[:, h * 256:(h + 1) * 256], wk[:, h * 256:(h + 1) * 256],
                   wv[:, h * 512:(h + 1) * 512], wg[:, h * 512:(h + 1) * 512]]
    wqkvg = np.ascontiguousarray(np.concatenate(blocks, axis=1))
    wo_vf = np.ascontiguousarray(np.asarray(ins["L0_attn_Wo"], np.float32).T)
    wq_mla = np.asarray(ins["L0_ffn_Wq"], np.float32)
    for h in range(MH):
        wq_mla = _perm_rope_cols(wq_mla, h * HD + NOPE, RP)
    wq_mla = np.ascontiguousarray(wq_mla)
    wkva = np.ascontiguousarray(_perm_rope_cols(np.asarray(ins["L0_ffn_Wkva"], np.float32),
                                                RLAT, RP))
    wkvb = np.ascontiguousarray(np.asarray(ins["L0_ffn_Wkvb"], np.float32))
    wo = np.asarray(ins["L0_ffn_Wo"], np.float32)
    woeff = np.ascontiguousarray(np.concatenate([wo[h * HD:h * HD + NOPE, :] for h in range(MH)],
                                                axis=0))
    wgate = np.ascontiguousarray(np.asarray(ins["L0_ffn_Wgate"], np.float32))
    bias8 = np.ascontiguousarray(np.asarray(ins["L0_ffn_expert_bias"], np.float32).reshape(1, NE))
    gu, dn = [], []
    for e in range(NE):
        gu.append(np.concatenate([np.asarray(ins["L0_ffn_Wg"][e], np.float32),
                                  np.asarray(ins["L0_ffn_Wu"][e], np.float32)], axis=1))
        dn.append(np.asarray(ins["L0_ffn_Wd"][e], np.float32))
    gu.append(np.concatenate([np.asarray(ins["L0_ffn_sWg"][0], np.float32),
                              np.asarray(ins["L0_ffn_sWu"][0], np.float32)], axis=1))
    dn.append(np.asarray(ins["L0_ffn_sWd"][0], np.float32))
    wgu = np.ascontiguousarray(np.concatenate(gu, axis=0))
    wd = np.ascontiguousarray(np.concatenate(dn, axis=0))
    wln1 = np.ascontiguousarray(np.asarray(ins["L0_ffn_w_ln1"], np.float32).reshape(8, 128).T)
    wln2 = np.ascontiguousarray(np.asarray(ins["L0_ffn_w_ln2"], np.float32).reshape(8, 128).T)
    gnw = np.ascontiguousarray(np.asarray(ins["L0_attn_gnorm_w"], np.float32).reshape(4, 128).T)
    return dict(wqkvg=wqkvg, wo_vf=wo_vf, wq_mla=wq_mla, wkva=wkva, wkvb=wkvb,
                woeff=woeff, wgate=wgate, bias8=bias8, wgu=wgu, wd=wd,
                wln1=wln1, wln2=wln2, gnw=gnw)


class _Sem:
    def __init__(self, es, nc, name):
        self.h = es.enter_context(nc.semaphore(name))
        self.n = 0

    def inc(self, instr, k=1):
        self.n += k
        instr.then_inc(self.h, k)
        return instr

    def wait(self, eng, n=None):
        eng.wait_ge(self.h, self.n if n is None else n)


class Builder:
    def __init__(self):
        self.nc = bass.Bass("TRN2", target_bir_lowering=False)
        self.es = ExitStack()
        nc = self.nc
        self.syn, self.act, self.vec, self.ten, self.gps = (
            nc.sync, nc.scalar, nc.vector, nc.tensor, nc.gpsimd)
        self._sems = {}

    def vflush(self):
        return self.vec.tensor_copy(out=self.flv[:, 0:1], in_=self.flv[:, 1:2])

    def aflush(self):
        return self.act.copy(out=self.fla[:, 0:1], in_=self.fla[:, 1:2])

    def gflush(self):
        return self.gps.memset(self.flg[:, 0:1], 0.0)

    def sem(self, name):
        s = _Sem(self.es, self.nc, name)
        self._sems[name] = s
        return s

    def fence(self):
        for eng in (self.syn, self.act, self.vec, self.ten, self.gps):
            for s in self._sems.values():
                eng.wait_ge(s.h, s.n)

    def sbuf(self, name, shape, dt, scope=None, **kw):
        cm = self.nc.sbuf_tensor(name, shape, dt, **kw)
        return (scope or self.es).enter_context(cm)

    def psum(self, name, shape, dt, scope=None):
        cm = self.nc.psum_tensor(name, shape, dt)
        return (scope or self.es).enter_context(cm)

    def build(self):
        nc = self.nc
        din = lambda n, s, d: nc.dram_tensor(n, s, d, kind="ExternalInput").ap()
        dint = lambda n, s, d: nc.dram_tensor(n, s, d, kind="Internal").ap()
        self.x0d = din("x0", [T, DM], F32)
        self.roled = din("role", [1, 1], I32)
        self.wqkvgd = din("wqkvg", [DM, RH * 1536], F32)
        self.wo_vfd = din("wo_vf", [2048, DM], F32)
        self.wqmlad = din("wq_mla", [DM, DM], F32)
        self.wkvad = din("wkva", [DM, 160], F32)
        self.wkvbd = din("wkvb", [RLAT, 512], F32)
        self.woeffd = din("woeff", [512, DM], F32)
        self.wgated = din("wgate", [DM, NE], F32)
        self.bias8d = din("bias8", [1, NE], F32)
        self.wgud = din("wgu", [9 * DM, DM], F32)
        self.wdd = din("wd", [9 * EI, DM], F32)
        self.wln1d = din("wln1", [128, 8], F32)
        self.wln2d = din("wln2", [128, 8], F32)
        self.gnwd = din("gnw", [128, 4], F32)
        self.cosfmd = din("cosfm", [128, T], BF16)
        self.sinfmd = din("sinfm", [128, T], BF16)
        self.masktd = din("maskt", [RH * SC, SC], F32)
        self.cdd = din("cd", [RH * 128, SC], BF16)
        self.kdecd = din("kdec", [128, 16], F32)
        self.mcosqd = din("mcosq", [128, 32 * 256], BF16)
        self.msinqd = din("msinq", [128, 32 * 256], BF16)
        self.mcoskd = din("mcosk", [128, 512], BF16)
        self.msinkd = din("msink", [128, 512], BF16)
        self.trid = din("tri", [128, 128], BF16)
        self.idbd = din("idb", [128, 128], BF16)
        self.idfd = din("idf", [128, 128], F32)

        self.ogd = dint("og_s", [2048, T], BF16)
        self.x1d = dint("x1_s", [T, DM], F32)
        self.std_ = dint("st_s", [T, 4], F32)
        self.qd = dint("q_s", [T, DM], BF16)
        self.y1d = dint("y1_s", [2048, DM], F32)
        self.x1cd = dint("x1c_s", [2048, DM], F32)
        self.outd = nc.dram_tensor("out", [2048, DM], F32, kind="ExternalOutput").ap()
        if _DUMP:
            self.dbgd = dint("dbg_s", [2048, T], BF16)
            self.dbg2d = dint("dbg2_s", [2048, T], BF16)
            self.dbg3d = dint("dbg3_s", [128, 32 * 528], BF16)
            self.dbg4d = dint("dbg4_s", [128, 64], F32)
            self.dbg5d = dint("dbg5_s", [128, 4096], BF16)

        self.srole = self.sbuf("srole", [1, 1], I32)
        self.flv = self.sbuf("flv", [128, 2], F32)
        self.fla = self.sbuf("fla", [128, 2], F32)
        self.flg = self.sbuf("flg", [128, 2], F32)
        self.cst = self.sbuf("cst", [128, 4], F32)
        self.wln = self.sbuf("wln", [128, 16], F32)
        self.gnw = self.sbuf("gnw_sb", [128, 4], F32)
        self.kdec = self.sbuf("kdec_sb", [128, 16], F32)
        self.tri = self.sbuf("tri_sb", [128, 128], BF16)
        self.idb = self.sbuf("idb_sb", [128, 128], BF16)
        self.idf = self.sbuf("idf_sb", [128, 128], F32)
        self.ones_f = self.sbuf("ones_f", [1, 128], F32)
        self.ones_b = self.sbuf("ones_b", [1, 128], BF16)
        self.bias_all = self.sbuf("bias_all", [128, 16], F32)
        self.biasg = self.sbuf("biasg", [128, 8], F32)
        self.b8row = self.sbuf("b8row", [1, 8], F32)
        self.mck = self.sbuf("mck", [128, 512], BF16)
        self.msk = self.sbuf("msk", [128, 512], BF16)
        self.nmax = self.sbuf("nmax", [128, 36], F32)
        self.stat = self.sbuf("stat", [128, 16], F32)

        self.ds = self.sem("ds")
        self.dg = self.sem("dg")
        self.gp = self.sem("gp")
        self.dmp = self.sem("dmp") if _DUMP else None

        self.p0()
        done = False
        with ExitStack() as z_scope:
            self.zfm = self.sbuf("zfm", [128, 8 * T], BF16, scope=z_scope,
                                 is_transpose_dest=True)
            self.cosfm = self.sbuf("cosfm_sb", [128, T], BF16, scope=z_scope)
            self.sinfm = self.sbuf("sinfm_sb", [128, T], BF16, scope=z_scope)
            self.prepass()
            if _DUMP:
                self.dz.wait(self.syn)
                for ci in range(8):
                    self.dmp.inc(self.syn.dma_start(
                        out=self.dbgd[ci * 128:(ci + 1) * 128, :],
                        in_=self.zfm[:, ci * T:(ci + 1) * T]), 16)
            if _LIMIT == 1:
                done = True
            if not done:
                self.s1a()
                if _LIMIT == 2:
                    done = True
        if not done:
            self.s1b()
            if _LIMIT == 3:
                done = True
        if not done:
            with ExitStack() as kv_scope:
                self.v_tm = self.sbuf("v_tm", [128, 32 * 528], BF16, scope=kv_scope)
                self.k_fm = self.sbuf("k_fm", [128, 8 * T], BF16, scope=kv_scope)
                self.p2a()
                if _LIMIT == 4:
                    done = True
                if not done:
                    self.p2b()
                    if _LIMIT == 5:
                        done = True
        if not done:
            self.p3()
        else:
            self.debug_out()
        if _DUMP:
            self.dmp.wait(self.syn)
        self.es.close()
        return self.nc

    def debug_out(self):
        syn = self.syn
        dfin = self.sem("dfin")
        self.fence()
        with self.nc.sbuf_tensor("dumout", [128, DM], F32) as du:
            self.gps.memset(du[:, :], 0.0)
            self.gp.inc(self.gflush())
            self.gp.wait(syn)
            for s in range(16):
                dfin.inc(syn.dma_start(out=self.outd[s * 128:(s + 1) * 128, :],
                                       in_=du[:, :]), 16)
            dfin.wait(syn)

    def p0(self):
        syn, gps = self.syn, self.gps
        for dst, src in [(self.srole[:], self.roled), (self.wln[:, 0:8], self.wln1d),
                         (self.wln[:, 8:16], self.wln2d), (self.gnw[:], self.gnwd),
                         (self.kdec[:], self.kdecd), (self.tri[:], self.trid),
                         (self.idb[:], self.idbd), (self.idf[:], self.idfd), (self.mck[:], self.mcoskd),
                         (self.msk[:], self.msinkd), (self.b8row[:], self.bias8d)]:
            self.ds.inc(syn.dma_start(out=dst, in_=src), 16)
        gps.memset(self.cst[:, 0:1], EPS)
        gps.memset(self.cst[:, 1:2], 1.0)
        gps.memset(self.ones_f[:, :], 1.0)
        gps.memset(self.ones_b[:, :], 1.0)
        gps.memset(self.nmax[:, :], 0.0)
        self.gp.inc(self.gflush())

    def prepass(self):
        nc, syn, act, vec = self.nc, self.syn, self.act, self.vec
        self.dz = self.sem("dz")
        ve0, ac0 = self.sem("ve0"), self.sem("ac0")
        stat, cst = self.stat, self.cst
        ds = self.ds
        ds.inc(syn.dma_start(out=self.cosfm[:], in_=self.cosfmd), 16)
        ds.inc(syn.dma_start(out=self.sinfm[:], in_=self.sinfmd), 16)
        with nc.sbuf_tensor("x0b", [128, 2 * DM], F32) as x0b, \
             nc.sbuf_tensor("ztm", [128, 2 * DM], BF16) as ztm, \
             nc.sbuf_tensor("sqs", [128, DM], BF16) as sqs:
            for it in range(T // 128):
                b = it % 2
                sb = 3 * (it % 2)
                if it >= 2:
                    ac0.wait(syn, 2 * (it - 1))
                    ve0.wait(syn, 2 * (it - 1))
                ds.inc(syn.dma_start(out=x0b[:, b * DM:(b + 1) * DM],
                                     in_=self.x0d[it * 128:(it + 1) * 128, :]), 16)
                ds.wait(act)
                self.gp.wait(act)
                ac0.inc(act.activation(out=sqs[:, :], in_=x0b[:, b * DM:(b + 1) * DM],
                                       func=AF.Square, accum_out=stat[:, sb:sb + 1]))
                ds.wait(vec)
                vec.reduce_sum(out=stat[:, sb + 1:sb + 2], in_=x0b[:, b * DM:(b + 1) * DM],
                               axis=AX.X)
                self.vflush()
                vec.tensor_scalar(out=stat[:, sb + 1:sb + 2], in0=stat[:, sb + 1:sb + 2],
                                  scalar1=1.0 / DM, scalar2=None, op0=ALU.mult)
                ac0.wait(vec)
                vec.tensor_scalar(out=stat[:, sb:sb + 1], in0=stat[:, sb:sb + 1],
                                  scalar1=1.0 / DM, scalar2=None, op0=ALU.mult)
                vec.tensor_tensor(out=stat[:, sb + 2:sb + 3], in0=stat[:, sb + 1:sb + 2],
                                  in1=stat[:, sb + 1:sb + 2], op=ALU.mult)
                self.vflush()
                ve0.inc(vec.tensor_tensor(out=stat[:, sb:sb + 1], in0=stat[:, sb:sb + 1],
                                          in1=stat[:, sb + 2:sb + 3], op=ALU.subtract))
                ve0.wait(act)
                act.activation(out=stat[:, sb + 2:sb + 3], in_=stat[:, sb:sb + 1],
                               func=AF.Ln, bias=cst[:, 0:1])
                ac0.inc(act.activation(out=stat[:, sb + 2:sb + 3], in_=stat[:, sb + 2:sb + 3],
                                       func=AF.Exp, scale=-0.5))
                ac0.wait(vec)
                if it >= 2:
                    self.dz.wait(vec, 16 * 8 * (it - 1))
                ve0.inc(vec.tensor_scalar(out=ztm[:, b * DM:(b + 1) * DM],
                                          in0=x0b[:, b * DM:(b + 1) * DM],
                                          scalar1=stat[:, sb + 1:sb + 2],
                                          scalar2=stat[:, sb + 2:sb + 3],
                                          op0=ALU.subtract, op1=ALU.mult))
                ve0.wait(syn)
                for ci in range(8):
                    self.dz.inc(syn.dma_start_transpose(
                        out=self.zfm[:, ci * T + it * 128:ci * T + (it + 1) * 128],
                        in_=ztm[:, b * DM + ci * 128:b * DM + (ci + 1) * 128]), 16)

    def s1a(self):
        if _STG != 0:
            self.fence()
        nc = self.nc
        syn, act, vec, ten, gps = self.syn, self.act, self.vec, self.ten, self.gps
        zfm, kdec, stat, cst = self.zfm, self.kdec, self.stat, self.cst

        pqk, pvgs, pat, po, psd = (self.sem("pqk"), self.sem("pvgs"), self.sem("pat"),
                                   self.sem("po"), self.sem("psd"))
        vqk, vrp, vkd, vvv, vat, vog, vsb = (self.sem("vqk"), self.sem("vrp"), self.sem("vkd"),
                                             self.sem("vvv"), self.sem("vat"), self.sem("vog"),
                                             self.sem("vsb"))
        ag, aor, ass = self.sem("ag"), self.sem("aor"), self.sem("ass")
        dktm, dogt = self.sem("dktm"), self.sem("dogt")
        self.dog = self.sem("dog")
        dog = self.dog
        ds, dg = self.ds, self.dg

        _st = ExitStack()
        s1w = _st.enter_context(nc.sbuf_tensor("s1w", [128, 8 * 1536], BF16))
        maskt = _st.enter_context(nc.sbuf_tensor("maskt_sb", [128, 4 * SC], F32))
        cdt = _st.enter_context(nc.sbuf_tensor("cdt", [128, SC], BF16))
        qkpre = _st.enter_context(nc.sbuf_tensor("qkpre", [128, 4 * SC], BF16))
        qrp = _st.enter_context(nc.sbuf_tensor("qrp", [128, 2 * SC], BF16))
        krp = _st.enter_context(nc.sbuf_tensor("krp", [128, 2 * SC], BF16))
        qtl = _st.enter_context(nc.sbuf_tensor("qtl", [128, 2 * SC], BF16))
        rsc = _st.enter_context(nc.sbuf_tensor("rsc", [128, 2 * SC], BF16))
        ktm = _st.enter_context(nc.sbuf_tensor("ktm", [128, 4 * DK], BF16, is_transpose_dest=True))
        kdk = _st.enter_context(nc.sbuf_tensor("kdk", [128, 4 * DK], BF16))
        vtm1 = _st.enter_context(nc.sbuf_tensor("vtm1", [128, 4 * DV], BF16))
        gsil = _st.enter_context(nc.sbuf_tensor("gsil", [128, 4 * DV], BF16))
        att = _st.enter_context(nc.sbuf_tensor("att", [128, 4 * SC], BF16))
        ogtm = _st.enter_context(nc.sbuf_tensor("ogtm", [128, 4 * DV], BF16))
        ogfm = _st.enter_context(nc.sbuf_tensor("ogfm", [128, 4 * SC], BF16, is_transpose_dest=True))
        o2s = _st.enter_context(nc.sbuf_tensor("o2s", [128, DV], BF16))
        og1 = _st.enter_context(nc.sbuf_tensor("og1", [128, DV], BF16))
        ssb = _st.enter_context(nc.sbuf_tensor("ssb", [128, 2 * DV], F32))
        sbf = _st.enter_context(nc.sbuf_tensor("sbf", [128, 2 * DV], BF16))
        pq = _st.enter_context(nc.psum_tensor("pq", [128, 2 * SC], F32))
        pk = _st.enter_context(nc.psum_tensor("pk", [128, 2 * SC], F32))
        pvg = _st.enter_context(nc.psum_tensor("pvg", [128, 4 * SC], F32))
        if True:

            Wq = lambda ci, dc: ci * 1536 + dc * 128
            Wk = lambda ci, dc: ci * 1536 + 256 + dc * 128
            Wv = lambda ci: ci * 1536 + 512
            Wg = lambda ci: ci * 1536 + 1024
            dwh, dsh = [], []

            for h in range(_S1A_H if _S1A_H else RH):
                if h > 0 and _STG >= 7:
                    psd.wait(gps, h * NSC)
                for ci in range(8):
                    dg.inc(gps.dma_start(out=s1w[:, ci * 1536:(ci + 1) * 1536],
                                         in_=self.wqkvgd[ci * 128:(ci + 1) * 128,
                                                         h * 1536:(h + 1) * 1536]), 16)
                dwh.append(dg.n)
                if h > 0 and _STG >= 4:
                    vat.wait(syn, 4 * h * NSC)
                if h > 0 and _STG >= 5:
                    po.wait(syn, 4 * h * NSC)
                for j in range(4):
                    ds.inc(syn.dma_start(out=maskt[:, j * SC:(j + 1) * SC],
                                         in_=self.masktd[h * SC + j * 128:h * SC + (j + 1) * 128, :]),
                           16)
                ds.inc(syn.dma_start(out=cdt[:], in_=self.cdd[h * 128:(h + 1) * 128, :]), 16)
                dsh.append(ds.n)
                gC = float(GAMMAS[h] ** SC)

                for sci in range((_S1A_S if _S1A_S else NSC) if _STG >= 1 else 0):
                    it = h * NSC + sci
                    tok0 = sci * SC
                    dg.wait(ten, dwh[h])
                    if it == 0:
                        self.dz.wait(ten, 16 * 8 * (T // 128))
                        self.gp.wait(ten)
                        ds.wait(ten, dsh[0])
                    else:
                        vqk.wait(ten, 4 * it)
                        if _STG >= 4:
                            vat.wait(ten, 4 * it)
                        if _STG >= 7:
                            vsb.wait(ten, it)
                    for dc in range(2):
                        for ci in range(8):
                            ten.matmul(pq[:, dc * SC:(dc + 1) * SC],
                                       s1w[:, Wq(ci, dc):Wq(ci, dc) + 128],
                                       zfm[:, ci * T + tok0:ci * T + tok0 + SC],
                                       start=(ci == 0), stop=(ci == 7))
                    for dc in range(2):
                        for ci in range(8):
                            mm_ = ten.matmul(pk[:, dc * SC:(dc + 1) * SC],
                                             s1w[:, Wk(ci, dc):Wk(ci, dc) + 128],
                                             zfm[:, ci * T + tok0:ci * T + tok0 + SC],
                                             start=(ci == 0), stop=(ci == 7))
                    pqk.inc(mm_)
                    for sub in range(4 if _STG >= 2 else 0):
                        bk = sub % 2
                        if sub < 2:
                            if it >= 1 and _STG >= 5:
                                vog.wait(ten, 4 * (it - 1) + 2 * (sub + 1))
                                aor.wait(ten, 4 * (it - 1) + 2 * (sub + 1))
                        else:
                            vvv.wait(ten, 4 * it + sub - 1)
                            ag.wait(ten, 4 * it + sub - 1)
                        for ci in range(8):
                            ten.matmul(pvg[:, (2 * bk) * SC:(2 * bk + 1) * SC],
                                       zfm[:, ci * T + tok0 + sub * 128:ci * T + tok0 + (sub + 1) * 128],
                                       s1w[:, Wv(ci):Wv(ci) + 512],
                                       start=(ci == 0), stop=(ci == 7))
                        for ci in range(8):
                            mm_ = ten.matmul(pvg[:, (2 * bk + 1) * SC:(2 * bk + 2) * SC],
                                             zfm[:, ci * T + tok0 + sub * 128:ci * T + tok0 + (sub + 1) * 128],
                                             s1w[:, Wg(ci):Wg(ci) + 512],
                                             start=(ci == 0), stop=(ci == 7))
                        pvgs.inc(mm_)
                    for j in range(4 if _STG >= 4 else 0):
                        bk = j % 2
                        if j < 2:
                            vqk.wait(ten, 4 * it + 2)
                            vrp.wait(ten, it + 1)
                        else:
                            vat.wait(ten, 4 * it + j - 1)
                        for dc in range(2):
                            mm_ = ten.matmul(pq[:, bk * SC:(bk + 1) * SC],
                                             krp[:, dc * SC + j * 128:dc * SC + (j + 1) * 128],
                                             qrp[:, dc * SC:(dc + 1) * SC],
                                             start=(dc == 0), stop=(dc == 1))
                        pat.inc(mm_)
                    for ii in range(4 if _STG >= 5 else 0):
                        vat.wait(ten, 4 * it + 4)
                        if ii < 2:
                            vvv.wait(ten, 4 * it + 3)
                            ag.wait(ten, 4 * it + 3)
                            if _STG >= 7:
                                vsb.wait(ten, it)
                        else:
                            vvv.wait(ten, 4 * it + 4)
                            ag.wait(ten, 4 * it + 4)
                        for j in range(4):
                            mm_ = ten.matmul(pvg[:, ii * SC:(ii + 1) * SC],
                                             att[:, j * SC + ii * 128:j * SC + (ii + 1) * 128],
                                             vtm1[:, j * DV:(j + 1) * DV],
                                             start=(j == 0),
                                             stop=(j == 3 and (sci == 0 or _STG < 7)))
                        if sci > 0 and _STG >= 7:
                            for dc in range(2):
                                mm_ = ten.matmul(pvg[:, ii * SC:(ii + 1) * SC],
                                                 qtl[:, dc * SC + ii * 128:dc * SC + (ii + 1) * 128],
                                                 sbf[:, dc * DV:(dc + 1) * DV],
                                                 start=False, stop=(dc == 1))
                        po.inc(mm_)
                    vqk.wait(ten, 4 * it + 4) if _STG >= 7 else None
                    vkd.wait(ten, it + 1) if _STG >= 7 else None
                    vvv.wait(ten, 4 * it + 4) if _STG >= 7 else None
                    for st_ in range(2 if _STG >= 7 else 0):
                        for sub in range(4):
                            mm_ = ten.matmul(pk[:, st_ * SC:(st_ + 1) * SC],
                                             kdk[:, sub * DK + st_ * 128:sub * DK + (st_ + 1) * 128],
                                             vtm1[:, sub * DV:(sub + 1) * DV],
                                             start=(sub == 0), stop=(sub == 3))
                    psd.inc(mm_) if _STG >= 7 else None

                    pqk.wait(vec, it + 1)
                    if it >= 1:
                        if _STG >= 4:
                            pat.wait(vec, 4 * it)
                        if _STG >= 3:
                            dktm.wait(vec, 128 * it)
                        if _STG >= 5:
                            po.wait(vec, 4 * it)
                        if _STG >= 7:
                            psd.wait(vec, it)
                    vqk.inc(vec.tensor_copy(out=qkpre[:, 0:SC], in_=pq[:, 0:SC]))
                    vqk.inc(vec.tensor_copy(out=qkpre[:, SC:2 * SC], in_=pq[:, SC:2 * SC]))
                    vqk.inc(vec.tensor_copy(out=qkpre[:, 2 * SC:3 * SC], in_=pk[:, 0:SC]))
                    vqk.inc(vec.tensor_copy(out=qkpre[:, 3 * SC:4 * SC], in_=pk[:, SC:2 * SC]))
                    cosw = self.cosfm[:, tok0:tok0 + SC]
                    sinw = self.sinfm[:, tok0:tok0 + SC]
                    for (s0, s1_, dst) in ([(qkpre[:, 0:SC], qkpre[:, SC:2 * SC], qrp),
                                            (qkpre[:, 2 * SC:3 * SC], qkpre[:, 3 * SC:4 * SC], krp)]
                                           if _R1 >= 1 else []):
                        vec.tensor_tensor(out=rsc[:, 0:SC], in0=s0, in1=cosw, op=ALU.mult)
                        vec.tensor_tensor(out=rsc[:, SC:2 * SC], in0=s1_, in1=sinw, op=ALU.mult)
                        vec.tensor_tensor(out=rsc[:, 0:SC], in0=rsc[:, 0:SC],
                                          in1=rsc[:, SC:2 * SC], op=ALU.subtract)
                        vec.tensor_tensor(out=rsc[:, SC:2 * SC], in0=s1_, in1=cosw, op=ALU.mult)
                        vec.tensor_tensor(out=dst[:, SC:2 * SC], in0=s0, in1=sinw, op=ALU.mult)
                        vec.tensor_tensor(out=dst[:, SC:2 * SC], in0=dst[:, SC:2 * SC],
                                          in1=rsc[:, SC:2 * SC], op=ALU.add)
                        vec.tensor_copy(out=dst[:, 0:SC], in_=rsc[:, 0:SC])
                    ds.wait(vec, dsh[h])
                    for dc in range(2 if _R1 >= 2 else 0):
                        vec.tensor_tensor(out=qtl[:, dc * SC:(dc + 1) * SC],
                                          in0=qrp[:, dc * SC:(dc + 1) * SC], in1=cdt[:],
                                          op=ALU.mult)
                    vrp.inc(self.vflush())
                    dktm.wait(vec, 128 * (it + 1)) if _STG >= 3 else None
                    for sub in range(4 if _STG >= 3 else 0):
                        vec.tensor_scalar(out=kdk[:, sub * DK:(sub + 1) * DK],
                                          in0=ktm[:, sub * DK:(sub + 1) * DK],
                                          scalar1=kdec[:, sub * 4 + h:sub * 4 + h + 1],
                                          scalar2=None, op0=ALU.mult)
                    vkd.inc(self.vflush()) if _STG >= 3 else None
                    for sub in range(4 if _STG >= 2 else 0):
                        bk = sub % 2
                        pvgs.wait(vec, 4 * it + sub + 1)
                        vvv.inc(vec.tensor_copy(out=vtm1[:, sub * DV:(sub + 1) * DV],
                                                in_=pvg[:, 2 * bk * SC:(2 * bk + 1) * SC]))
                    for j in range(4 if _STG >= 4 else 0):
                        bk = j % 2
                        pat.wait(vec, 4 * it + j + 1)
                        vat.inc(vec.tensor_tensor(out=att[:, j * SC:(j + 1) * SC],
                                                  in0=pq[:, bk * SC:(bk + 1) * SC],
                                                  in1=maskt[:, j * SC:(j + 1) * SC], op=ALU.mult))
                    if it >= 1 and _STG >= 6:
                        dogt.wait(vec, 256 * it)
                    for ii in range(4 if _STG >= 5 else 0):
                        aor.wait(vec, 4 * it + ii + 1)
                        pc = 9 + (ii % 2) * 2
                        vec.tensor_scalar(out=og1[:, :], in0=pvg[:, ii * SC:(ii + 1) * SC],
                                          scalar1=stat[:, pc:pc + 1], scalar2=None, op0=ALU.mult)
                        vog.inc(vec.tensor_tensor(out=ogtm[:, ii * DV:(ii + 1) * DV],
                                                  in0=og1[:, :], in1=gsil[:, ii * DV:(ii + 1) * DV],
                                                  op=ALU.mult))
                    if _STG >= 7:
                        ass.wait(vec, 2 * (it + 1))
                        psd.wait(vec, it + 1)
                        if sci == 0:
                            vec.tensor_copy(out=ssb[:, 0:DV], in_=pk[:, 0:SC])
                            vec.tensor_copy(out=ssb[:, DV:2 * DV], in_=pk[:, SC:2 * SC])
                        else:
                            vec.tensor_tensor(out=ssb[:, 0:DV], in0=ssb[:, 0:DV],
                                              in1=pk[:, 0:SC], op=ALU.add)
                            vec.tensor_tensor(out=ssb[:, DV:2 * DV], in0=ssb[:, DV:2 * DV],
                                              in1=pk[:, SC:2 * SC], op=ALU.add)
                        vec.tensor_copy(out=sbf[:, 0:DV], in_=ssb[:, 0:DV])
                        vec.tensor_copy(out=sbf[:, DV:2 * DV], in_=ssb[:, DV:2 * DV])
                    vsb.inc(self.vflush())

                    if _STG >= 3:
                        vrp.wait(syn, it + 1)
                    if it >= 1 and _STG >= 3:
                        vkd.wait(syn, it)
                    for dc in range(2 if _STG >= 3 else 0):
                        eng_ = syn
                        for sub in range(4):
                            dktm.inc(eng_.dma_start_transpose(
                                out=ktm[:, sub * DK + dc * 128:sub * DK + (dc + 1) * 128],
                                in_=krp[:, dc * SC + sub * 128:dc * SC + (sub + 1) * 128]), 16)
                    if it >= 1 and _STG >= 5:
                        vog.wait(act, 4 * it)
                    for sub in range(4 if _STG >= 2 else 0):
                        bk = sub % 2
                        pvgs.wait(act, 4 * it + sub + 1)
                        ag.inc(act.activation(out=gsil[:, sub * DV:(sub + 1) * DV],
                                              in_=pvg[:, (2 * bk + 1) * SC:(2 * bk + 2) * SC],
                                              func=AF.Silu))
                    for ii in range(4 if _STG >= 5 else 0):
                        po.wait(act, 4 * it + ii + 1)
                        if 4 * it + ii >= 2:
                            vog.wait(act, 4 * it + ii - 1)
                        pc = 9 + (ii % 2) * 2
                        act.activation(out=o2s[:, :], in_=pvg[:, ii * SC:(ii + 1) * SC],
                                       func=AF.Square, accum_out=stat[:, pc - 1:pc])
                        act.activation(out=stat[:, pc:pc + 1], in_=stat[:, pc - 1:pc],
                                       func=AF.Ln, scale=1.0 / DV, bias=cst[:, 0:1])
                        aor.inc(act.activation(out=stat[:, pc:pc + 1], in_=stat[:, pc:pc + 1],
                                               func=AF.Exp, scale=-0.5))
                    if _STG >= 6:
                        vog.wait(syn, 4 * it + 4)
                    if it >= 1 and _STG >= 6:
                        dog.wait(syn, 64 * it)
                    for vc in range(4 if _STG >= 6 else 0):
                        eng_ = syn
                        for tt in range(4):
                            dogt.inc(eng_.dma_start_transpose(
                                out=ogfm[:, vc * SC + tt * 128:vc * SC + (tt + 1) * 128],
                                in_=ogtm[:, tt * DV + vc * 128:tt * DV + (vc + 1) * 128]), 16)
                    if sci == 0 or _STG < 7:
                        if _STG >= 7:
                            ass.inc(self.aflush())
                            ass.inc(self.aflush())
                    else:
                        vsb.wait(act, it)
                        ass.inc(act.activation(out=ssb[:, 0:DV], in_=ssb[:, 0:DV],
                                                func=AF.Copy, scale=gC))
                        ass.inc(act.activation(out=ssb[:, DV:2 * DV], in_=ssb[:, DV:2 * DV],
                                               func=AF.Copy, scale=gC))

                    if _STG >= 6:
                        dogt.wait(syn, 256 * (it + 1))
                    for vc in range(4 if _STG >= 6 else 0):
                        dog.inc(syn.dma_start(
                            out=self.ogd[h * 512 + vc * 128:h * 512 + (vc + 1) * 128,
                                         tok0:tok0 + SC],
                            in_=ogfm[:, vc * SC:(vc + 1) * SC]), 16)

            self.s1a_tot = dict(psd=psd.n, vsb=vsb.n, dog=dog.n, vat=vat.n, po=po.n,
                                aor=aor.n, vog=vog.n, ag=ag.n, vvv=vvv.n, vqk=vqk.n,
                                pat=pat.n, pqk=pqk.n, pvgs=pvgs.n, ass=ass.n,
                                vkd=vkd.n, vrp=vrp.n, dktm=dktm.n, dogt=dogt.n)
        _st.close()

    def s1b(self):
        self.fence()
        nc = self.nc
        syn, act, vec, ten, gps = self.syn, self.act, self.vec, self.ten, self.gps
        stat, cst = self.stat, self.cst
        ds, dg = self.ds, self.dg
        pr_s = self.sem("pr1b")
        vr_s = self.sem("vr1b")
        ar_s = self.sem("ar1b")
        dx_s = self.sem("dx1b")
        dsp = self.sem("dsp1b")
        dgw = self.sem("dgw1b")
        vwo = self.sem("vwo1b")

        with nc.sbuf_tensor("wo_sb", [128, 16 * DM], BF16) as wo_sb, \
             nc.sbuf_tensor("ogl", [128, 2 * 16 * SC], BF16) as ogl, \
             nc.sbuf_tensor("x0c", [128, 2 * DM], F32) as x0c, \
             nc.sbuf_tensor("rsb", [128, 2 * DM], F32) as rsb, \
             nc.sbuf_tensor("x1sb", [128, 2 * DM], F32) as x1sb, \
             nc.sbuf_tensor("sq1", [128, DM], BF16) as sq1, \
             nc.psum_tensor("pr", [128, 4 * SC], F32) as pr:

            self.psd_final = self.s1a_tot["psd"]
            for vc in range(16):
                dgw.inc(gps.dma_start(out=wo_sb[:, vc * DM:(vc + 1) * DM],
                                      in_=self.wo_vfd[vc * 128:(vc + 1) * 128, :]), 16)
            dgw.wait(vec)
            for vc in range(16):
                vec.tensor_scalar(out=wo_sb[:, vc * DM:(vc + 1) * DM],
                                  in0=wo_sb[:, vc * DM:(vc + 1) * DM],
                                  scalar1=self.gnw[:, vc % 4:vc % 4 + 1], scalar2=None,
                                  op0=ALU.mult)
            vwo.inc(self.vflush())

            vr_land = []
            ar_land = []
            for tt in range(T // 128):
                b = tt % 2
                ch = tt // 4
                if tt % 4 == 0:
                    bb = ch % 2
                    if tt == 0:
                        self.dog.wait(syn, self.s1a_tot["dog"])
                    if ch >= 2:
                        pr_s.wait(syn, 4 * (ch - 1))
                    for vc in range(16):
                        dx_s.inc(syn.dma_start(
                            out=ogl[:, bb * 16 * SC + vc * SC:bb * 16 * SC + (vc + 1) * SC],
                            in_=self.ogd[vc * 128:(vc + 1) * 128, ch * SC:(ch + 1) * SC]), 16)
                if tt >= 2:
                    vr_s.wait(syn, vr_land[tt - 2])
                dx_s.inc(syn.dma_start(out=x0c[:, b * DM:(b + 1) * DM],
                                       in_=self.x0d[tt * 128:(tt + 1) * 128, :]), 16)
                bb = ch % 2
                sub = tt % 4
                dx_s.wait(ten)
                vwo.wait(ten)
                if tt >= 2:
                    vr_s.wait(ten, vr_land[tt - 2])
                for fh in range(2):
                    for vc in range(16):
                        mm_ = ten.matmul(pr[:, (2 * b + fh) * SC:(2 * b + fh + 1) * SC],
                                         ogl[:, bb * 16 * SC + vc * SC + sub * 128:
                                             bb * 16 * SC + vc * SC + (sub + 1) * 128],
                                         wo_sb[:, vc * DM + fh * SC:vc * DM + (fh + 1) * SC],
                                         start=(vc == 0), stop=(vc == 15))
                pr_s.inc(mm_)
                pr_s.wait(vec, tt + 1)
                vec.tensor_copy(out=rsb[:, b * DM:b * DM + SC], in_=pr[:, 2 * b * SC:(2 * b + 1) * SC])
                vr_s.inc(vec.tensor_copy(out=rsb[:, b * DM + SC:(b + 1) * DM],
                                         in_=pr[:, (2 * b + 1) * SC:(2 * b + 2) * SC]))
                vr_s.wait(act)
                ar_s.inc(act.activation(out=sq1[:, :], in_=rsb[:, b * DM:(b + 1) * DM],
                                        func=AF.Square, accum_out=stat[:, 0:1]))
                vec.reduce_sum(out=stat[:, 1:2], in_=rsb[:, b * DM:(b + 1) * DM], axis=AX.X)
                self.vflush()
                vec.tensor_scalar(out=stat[:, 1:2], in0=stat[:, 1:2], scalar1=1.0 / DM,
                                  scalar2=None, op0=ALU.mult)
                ar_s.wait(vec)
                vec.tensor_scalar(out=stat[:, 0:1], in0=stat[:, 0:1], scalar1=1.0 / DM,
                                  scalar2=None, op0=ALU.mult)
                vec.tensor_tensor(out=stat[:, 2:3], in0=stat[:, 1:2], in1=stat[:, 1:2],
                                  op=ALU.mult)
                self.vflush()
                vr_s.inc(vec.tensor_tensor(out=stat[:, 0:1], in0=stat[:, 0:1], in1=stat[:, 2:3],
                                           op=ALU.subtract))
                vr_s.wait(act)
                act.activation(out=stat[:, 2:3], in_=stat[:, 0:1], func=AF.Ln, bias=cst[:, 0:1])
                ar_s.inc(act.activation(out=stat[:, 2:3], in_=stat[:, 2:3], func=AF.Exp,
                                        scale=-0.5))
                ar_s.wait(vec)
                vec.tensor_scalar(out=x1sb[:, b * DM:(b + 1) * DM], in0=rsb[:, b * DM:(b + 1) * DM],
                                  scalar1=stat[:, 1:2], scalar2=stat[:, 2:3],
                                  op0=ALU.subtract, op1=ALU.mult)
                vr_s.inc(vec.tensor_tensor(out=x1sb[:, b * DM:(b + 1) * DM],
                                           in0=x1sb[:, b * DM:(b + 1) * DM],
                                           in1=x0c[:, b * DM:(b + 1) * DM], op=ALU.add))
                vr_s.wait(syn)
                dsp.inc(syn.dma_start(out=self.x1d[tt * 128:(tt + 1) * 128, :],
                                      in_=x1sb[:, b * DM:(b + 1) * DM]), 16)
                vr_s.wait(act)
                ar_s.inc(act.activation(out=sq1[:, :], in_=x1sb[:, b * DM:(b + 1) * DM],
                                        func=AF.Square, accum_out=stat[:, 0:1]))
                vec.reduce_sum(out=stat[:, 1:2], in_=x1sb[:, b * DM:(b + 1) * DM], axis=AX.X)
                self.vflush()
                vec.tensor_scalar(out=stat[:, 1:2], in0=stat[:, 1:2], scalar1=1.0 / DM,
                                  scalar2=None, op0=ALU.mult)
                ar_s.wait(vec)
                vec.tensor_scalar(out=stat[:, 0:1], in0=stat[:, 0:1], scalar1=1.0 / DM,
                                  scalar2=None, op0=ALU.mult)
                vec.tensor_tensor(out=stat[:, 2:3], in0=stat[:, 1:2], in1=stat[:, 1:2],
                                  op=ALU.mult)
                self.vflush()
                vr_s.inc(vec.tensor_tensor(out=stat[:, 0:1], in0=stat[:, 0:1],
                                           in1=stat[:, 2:3], op=ALU.subtract))
                vr_s.wait(act)
                act.activation(out=stat[:, 3:4], in_=stat[:, 0:1], func=AF.Ln, bias=cst[:, 0:1])
                ar_s.inc(act.activation(out=stat[:, 3:4], in_=stat[:, 3:4], func=AF.Exp,
                                        scale=-0.5))
                ar_s.wait(vec)
                vec.tensor_scalar(out=stat[:, 5:6], in0=stat[:, 0:1], scalar1=cst[:, 0:1],
                                  scalar2=None, op0=ALU.add)
                self.vflush()
                vec.reciprocal(out=stat[:, 6:7], in_=stat[:, 5:6])
                self.vflush()
                vr_s.inc(vec.tensor_tensor(out=stat[:, 6:7], in0=stat[:, 6:7], in1=stat[:, 0:1],
                                           op=ALU.mult))
                vr_s.wait(act)
                act.activation(out=stat[:, 7:8], in_=stat[:, 6:7], func=AF.Ln, bias=cst[:, 0:1])
                ar_s.inc(act.activation(out=stat[:, 7:8], in_=stat[:, 7:8], func=AF.Exp,
                                        scale=-0.5))
                ar_s.wait(vec)
                vec.tensor_copy(out=stat[:, 12:13], in_=stat[:, 1:2])
                vec.tensor_copy(out=stat[:, 13:14], in_=stat[:, 3:4])
                vr_s.inc(vec.tensor_tensor(out=stat[:, 14:15], in0=stat[:, 3:4],
                                           in1=stat[:, 7:8], op=ALU.mult))
                vr_s.wait(syn)
                dsp.inc(syn.dma_start(out=self.std_[tt * 128:(tt + 1) * 128, 0:3],
                                      in_=stat[:, 12:15]), 16)
                vr_land.append(vr_s.n)
                ar_land.append(ar_s.n)
            self.s1b_dsp = dsp.n

    def p2a(self):
        self.fence()
        nc = self.nc
        syn, act, vec, ten, gps = self.syn, self.act, self.vec, self.ten, self.gps
        stat, cst, wln, kfm, vtm = self.stat, self.cst, self.wln, self.k_fm, self.v_tm
        d2x = self.sem("d2x")
        d2q = self.sem("d2q")
        dyn = self.sem("dyn")
        p2p = self.sem("p2p")
        v2p = self.sem("v2p")
        v2m = self.sem("v2m")
        a2p = self.sem("a2p")
        dgw2 = self.sem("dgw2")

        _st = ExitStack()
        wqm = _st.enter_context(nc.sbuf_tensor("wqm", [128, 8 * DM], BF16))
        wkva_sb = _st.enter_context(nc.sbuf_tensor("wkva_sb", [128, 8 * 160], BF16))
        wkvb_sb = _st.enter_context(nc.sbuf_tensor("wkvb_sb", [128, 512], BF16))
        x1t = _st.enter_context(nc.sbuf_tensor("x1t", [128, 2 * DM], F32))
        stt = _st.enter_context(nc.sbuf_tensor("stt", [128, 2 * 4], F32))
        yntm = _st.enter_context(nc.sbuf_tensor("yntm", [128, 2 * DM], BF16))
        ynfm = _st.enter_context(nc.sbuf_tensor("ynfm", [128, 2 * 1024], BF16, is_transpose_dest=True))
        qtm = _st.enter_context(nc.sbuf_tensor("qtm", [128, 2 * DM], BF16))
        qsc = _st.enter_context(nc.sbuf_tensor("qsc", [128, 2 * 256], BF16))
        mcq = _st.enter_context(nc.sbuf_tensor("mcq", [128, 2 * 256], BF16))
        msq = _st.enter_context(nc.sbuf_tensor("msq", [128, 2 * 256], BF16))
        latfm = _st.enter_context(nc.sbuf_tensor("latfm", [128, 128], BF16))
        krf = _st.enter_context(nc.sbuf_tensor("krf", [32, 128], BF16))
        krtm = _st.enter_context(nc.sbuf_tensor("krtm", [128, 32], BF16))
        krt2 = _st.enter_context(nc.sbuf_tensor("krt2", [128, 32], BF16))
        krsc = _st.enter_context(nc.sbuf_tensor("krsc", [128, 64], BF16))
        sqn = _st.enter_context(nc.sbuf_tensor("sqn", [128, DM], BF16))
        gr16 = _st.enter_context(nc.sbuf_tensor("gr16", [128, 16], F32))
        r2c = _st.enter_context(nc.sbuf_tensor("r2c", [128, 1], F32))
        pq2 = _st.enter_context(nc.psum_tensor("pq2", [128, 2 * SC], F32))
        plat = _st.enter_context(nc.psum_tensor("plat", [128, SC], F32))
        prp = _st.enter_context(nc.psum_tensor("prp", [128, SC], F32))
        pnp = _st.enter_context(nc.psum_tensor("pnp", [128, SC], F32))
        pvt = _st.enter_context(nc.psum_tensor("pvt", [128, SC], F32))
        ptrb = _st.enter_context(nc.psum_tensor("ptrb", [128, 128], BF16))
        if True:

            gps.memset(vtm.ap().rearrange("p (s h c) -> p s h c", h=MH, c=33)[:, :, :, 32:33],
                       1.0)
            self.gp.inc(self.gflush())
            for ci in range(8):
                dgw2.inc(gps.dma_start(out=wqm[:, ci * DM:(ci + 1) * DM],
                                       in_=self.wqmlad[ci * 128:(ci + 1) * 128, :]), 16)
            for ci in range(8):
                dgw2.inc(gps.dma_start(out=wkva_sb[:, ci * 160:(ci + 1) * 160],
                                       in_=self.wkvad[ci * 128:(ci + 1) * 128, :]), 16)
            dgw2.inc(gps.dma_start(out=wkvb_sb[:], in_=self.wkvbd), 16)

            vland, aland, pland = [], [], []
            for ts in range(32):
                bs = ts % 2
                t0 = ts * 128
                if ts >= 2:
                    v2p.wait(syn, vland[ts - 2])
                    a2p.wait(syn, aland[ts - 2])
                d2x.inc(syn.dma_start(out=x1t[:, bs * DM:(bs + 1) * DM],
                                      in_=self.x1d[t0:t0 + 128, :]), 16)
                d2x.inc(syn.dma_start(out=stt[:, bs * 4:bs * 4 + 3],
                                      in_=self.std_[t0:t0 + 128, 0:3]), 16)
                d2x.inc(syn.dma_start(out=mcq[:, bs * 256:(bs + 1) * 256],
                                      in_=self.mcosqd[:, ts * 256:(ts + 1) * 256]), 16)
                d2x.inc(syn.dma_start(out=msq[:, bs * 256:(bs + 1) * 256],
                                      in_=self.msinqd[:, ts * 256:(ts + 1) * 256]), 16)
                d2x.wait(vec)
                if ts == 0:
                    self.ds.wait(vec)
                vec.tensor_scalar(out=yntm[:, bs * DM:(bs + 1) * DM],
                                  in0=x1t[:, bs * DM:(bs + 1) * DM],
                                  scalar1=stt[:, bs * 4:bs * 4 + 1],
                                  scalar2=stt[:, bs * 4 + 2:bs * 4 + 3],
                                  op0=ALU.subtract, op1=ALU.mult)
                v2m.inc(self.vflush())
                if _DUMP and ts < 2:
                    v2m.wait(syn)
                    self.dmp.inc(syn.dma_start(out=self.dbg5d[:, ts * 1024:(ts + 1) * 1024],
                                               in_=yntm[:, bs * DM:(bs + 1) * DM]), 16)
                v2m.wait(act)
                v2m.wait(syn)
                if ts >= 2:
                    p2p.wait(act, pland[ts - 2])
                    p2p.wait(syn, pland[ts - 2])
                for ci in range(8):
                    eng_ = act if ci < 4 else syn
                    dyn.inc(eng_.dma_start_transpose(
                        out=ynfm[:, bs * 1024 + ci * 128:bs * 1024 + (ci + 1) * 128],
                        in_=yntm[:, bs * DM + ci * 128:bs * DM + (ci + 1) * 128]), 16)
                dyn.wait(vec)
                for ci in range(8):
                    vec.tensor_scalar(out=ynfm[:, bs * 1024 + ci * 128:bs * 1024 + (ci + 1) * 128],
                                      in0=ynfm[:, bs * 1024 + ci * 128:bs * 1024 + (ci + 1) * 128],
                                      scalar1=wln[:, ci:ci + 1], scalar2=None, op0=ALU.mult)
                v2m.inc(self.vflush())
                if _DUMP and ts < 2:
                    v2m.wait(syn)
                    self.dmp.inc(syn.dma_start(
                        out=self.dbg5d[:, 2048 + ts * 1024:2048 + (ts + 1) * 1024],
                        in_=ynfm[:, bs * 1024:(bs + 1) * 1024]), 16)
                v2m.wait(ten)
                if ts == 0:
                    dgw2.wait(ten)
                if ts >= 1:
                    v2p.wait(ten, vland[ts - 1])
                    a2p.wait(ten, aland[ts - 1])
                yn = lambda ci: ynfm[:, bs * 1024 + ci * 128:bs * 1024 + (ci + 1) * 128]
                for fh in range(2):
                    for ci in range(8):
                        ten.matmul(pq2[:, fh * SC:(fh + 1) * SC], yn(ci),
                                   wqm[:, ci * DM + fh * SC:ci * DM + (fh + 1) * SC],
                                   start=(ci == 0), stop=(ci == 7))
                for ci in range(8):
                    ten.matmul(plat[:, 0:128],
                               wkva_sb[:, ci * 160:ci * 160 + 128], yn(ci),
                               start=(ci == 0), stop=(ci == 7))
                for ci in range(8):
                    mm_ = ten.matmul(prp[0:32, 0:128],
                                     wkva_sb[:, ci * 160 + 128:ci * 160 + 160], yn(ci),
                                     start=(ci == 0), stop=(ci == 7))
                p2p.inc(mm_)
                p2p.wait(vec)
                if ts >= 2:
                    d2q.wait(vec, 16 * (ts - 1))
                vec.tensor_copy(out=qtm[:, bs * DM:bs * DM + SC], in_=pq2[:, 0:SC])
                vec.tensor_copy(out=qtm[:, bs * DM + SC:(bs + 1) * DM], in_=pq2[:, SC:2 * SC])
                vec.tensor_copy(out=latfm[:, :], in_=plat[:, 0:128])
                vec.tensor_copy(out=krf[0:32, :], in_=prp[0:32, 0:128])
                v2m.inc(self.vflush())
                v2m.wait(ten)
                ten.matmul(pvt[:, 0:SC], latfm[:, :], wkvb_sb[:, :], start=True, stop=True)
                for hh in range(4):
                    ten.matmul(pnp[0:32, hh * 128:(hh + 1) * 128],
                               wkvb_sb[:, (hh) * 32:(hh + 1) * 32], latfm[:, :],
                               start=True, stop=True)
                mm_ = ten.transpose(ptrb[0:128, 0:32], krf[0:32, :], self.idb[0:32, 0:32])
                p2p.inc(mm_)
                p2p.wait(vec)
                vec.tensor_copy(out=krtm[:, :], in_=ptrb[0:128, 0:32])
                v2m.inc(self.vflush())
                v2m.wait(act)
                act.activation(out=sqn[:, 0:32], in_=krtm[:, :], func=AF.Square,
                               accum_out=r2c[:, 0:1])
                a2p.inc(self.aflush())
                kc = self.mck[:, ts * 16:(ts + 1) * 16]
                ks = self.msk[:, ts * 16:(ts + 1) * 16]
                x0v, x1v = krtm[:, 0:16], krtm[:, 16:32]
                a2p.wait(vec)
                vec.tensor_tensor(out=krsc[:, 0:16], in0=x0v, in1=kc, op=ALU.mult)
                vec.tensor_tensor(out=krsc[:, 16:32], in0=x1v, in1=ks, op=ALU.mult)
                vec.tensor_tensor(out=krsc[:, 32:48], in0=x0v, in1=ks, op=ALU.mult)
                vec.tensor_tensor(out=krsc[:, 48:64], in0=x1v, in1=kc, op=ALU.mult)
                vec.tensor_tensor(out=krt2[:, 0:16], in0=krsc[:, 0:16], in1=krsc[:, 16:32],
                                  op=ALU.subtract)
                vec.tensor_tensor(out=krt2[:, 16:32], in0=krsc[:, 32:48], in1=krsc[:, 48:64],
                                  op=ALU.add)
                v2m.inc(self.vflush())
                v2m.wait(ten)
                mm_ = ten.transpose(ptrb[0:32, 0:128], krt2[:, :], self.idb[:, :])
                p2p.inc(mm_)
                p2p.wait(vec)
                for pr_ in range(8):
                    col = pr_ * T + t0
                    vec.tensor_copy(out=kfm[32:64, col:col + 128], in_=ptrb[0:32, 0:128])
                    vec.tensor_copy(out=kfm[96:128, col:col + 128], in_=ptrb[0:32, 0:128])
                for hb in range(4):
                    if hb > 0:
                        v2m.wait(ten)
                    for hh in range(4):
                        h = hb * 4 + hh
                        pr_, rb = h // 2, (h % 2) * 64
                        vec.tensor_copy(out=kfm[rb:rb + 32, pr_ * T + t0:pr_ * T + t0 + 128],
                                        in_=pnp[0:32, hh * 128:(hh + 1) * 128])
                    if hb < 3:
                        v2m.inc(self.vflush())
                        v2m.wait(ten)
                        for hh in range(4):
                            h = (hb + 1) * 4 + hh
                            mm_ = ten.matmul(pnp[0:32, hh * 128:(hh + 1) * 128],
                                             wkvb_sb[:, h * 32:(h + 1) * 32], latfm[:, :],
                                             start=True, stop=True)
                        p2p.inc(mm_)
                        p2p.wait(vec)
                if ts == 0:
                    self.gp.wait(vec)
                vec.tensor_copy(
                    out=vtm.ap().rearrange("p (s h c) -> p s h c", h=MH, c=33)[:, ts, :, 0:32],
                    in_=pvt.ap().rearrange("p (h c) -> p h c", c=32)[:, 0:MH, :])
                v2m.inc(self.vflush())
                qq = qtm[:, bs * DM:(bs + 1) * DM].rearrange("p (h d) -> p h d", d=HD)
                q0 = qq[:, :, NOPE:NOPE + 16]
                q1 = qq[:, :, NOPE + 16:NOPE + 32]
                c3 = mcq[:, bs * 256:(bs + 1) * 256].rearrange("p (h f) -> p h f", f=16)
                s3 = msq[:, bs * 256:(bs + 1) * 256].rearrange("p (h f) -> p h f", f=16)
                qs0 = qsc[:, 0:256].rearrange("p (h f) -> p h f", f=16)
                qs1 = qsc[:, 256:512].rearrange("p (h f) -> p h f", f=16)
                vec.tensor_tensor(out=qs0, in0=q0, in1=c3, op=ALU.mult)
                vec.tensor_tensor(out=qs1, in0=q1, in1=s3, op=ALU.mult)
                vec.tensor_tensor(out=qs1, in0=qs0, in1=qs1, op=ALU.subtract)
                vec.tensor_tensor(out=qs0, in0=q0, in1=s3, op=ALU.mult)
                vec.tensor_tensor(out=q1, in0=q1, in1=c3, op=ALU.mult)
                vec.tensor_tensor(out=q1, in0=q1, in1=qs0, op=ALU.add)
                vec.tensor_copy(out=q0, in_=qs1)
                v2m.inc(self.vflush())
                v2m.wait(syn)
                d2q.inc(syn.dma_start(out=self.qd[t0:t0 + 128, :],
                                      in_=qtm[:, bs * DM:(bs + 1) * DM]), 16)
                v2m.wait(act)
                act.activation(out=sqn[:, :], in_=qtm[:, bs * DM:(bs + 1) * DM], func=AF.Square)
                a2p.inc(self.aflush())
                a2p.wait(vec)
                vec.reduce_sum(out=gr16[:, :],
                               in_=sqn.ap().rearrange("p (h d) -> p h d", d=HD), axis=AX.X)
                self.vflush()
                vec.tensor_tensor(out=self.nmax[:, 0:16], in0=self.nmax[:, 0:16],
                                  in1=gr16[:, :], op=ALU.max)
                v2m.inc(self.vflush())
                v2m.wait(act)
                act.activation(out=sqn[:, 0:512], in_=pvt[:, 0:SC], func=AF.Square)
                a2p.inc(self.aflush())
                a2p.wait(vec)
                vec.reduce_sum(out=gr16[:, :],
                               in_=sqn[:, 0:512].rearrange("p (h d) -> p h d", d=32), axis=AX.X)
                self.vflush()
                vec.tensor_scalar(out=gr16[:, :], in0=gr16[:, :], scalar1=r2c[:, 0:1],
                                  scalar2=None, op0=ALU.add)
                self.vflush()
                vec.tensor_tensor(out=self.nmax[:, 16:32], in0=self.nmax[:, 16:32],
                                  in1=gr16[:, :], op=ALU.max)
                v2p.inc(self.vflush())
                vland.append(v2p.n)
                aland.append(a2p.n)
                pland.append(p2p.n)

            v2p.wait(ten)
            a2p.wait(ten)
            ten.transpose(pq2[0:16, 0:128], self.nmax[:, 0:16], self.idf[:, :])
            mm_ = ten.transpose(pq2[0:16, SC:SC + 128], self.nmax[:, 16:32], self.idf[:, :])
            p2p.inc(mm_)
            p2p.wait(vec)
            vec.reduce_max(out=self.nmax[0:16, 32:33], in_=pq2[0:16, 0:128], axis=AX.X)
            vec.reduce_max(out=self.nmax[0:16, 33:34], in_=pq2[0:16, SC:SC + 128], axis=AX.X)
            self.vflush()
            vec.tensor_tensor(out=self.nmax[0:16, 34:35], in0=self.nmax[0:16, 32:33],
                              in1=self.nmax[0:16, 33:34], op=ALU.mult)
            v2p.inc(self.vflush())
            v2p.wait(act)
            act.activation(out=self.nmax[0:16, 34:35], in_=self.nmax[0:16, 34:35], func=AF.Sqrt)
            a2p.inc(self.aflush())
            a2p.wait(vec)
            vec.tensor_scalar(out=self.nmax[0:16, 35:36], in0=self.nmax[0:16, 34:35],
                              scalar1=-0.1275, scalar2=None, op0=ALU.mult)
            v2p.inc(self.vflush())
            v2p.wait(ten)
            mm_ = ten.transpose(prp[0:1, 0:16], self.nmax[0:16, 35:36], self.idf[0:16, 0:16])
            p2p.inc(mm_)
            p2p.wait(vec)
            vec.tensor_copy(out=gr16[0:1, :], in_=prp[0:1, 0:16])
            v2p.inc(self.vflush())
            v2p.wait(ten)
            ten.matmul(pnp[:, 0:16], self.ones_f[0:1, :], gr16[0:1, 0:16], start=True, stop=True)
            mm_ = ten.matmul(pnp[:, 16:24], self.ones_f[0:1, :], self.b8row[0:1, :],
                             start=True, stop=True)
            p2p.inc(mm_)
            p2p.wait(vec)
            vec.tensor_copy(out=self.bias_all[:, :], in_=pnp[:, 0:16])
            vec.tensor_copy(out=self.biasg[:, :], in_=pnp[:, 16:24])
            v2p.inc(self.vflush())
            self.p2a_tot = dict(v2p=v2p.n, a2p=a2p.n, p2p=p2p.n, d2q=d2q.n)
        _st.close()

    def p2b(self):
        nc = self.nc
        syn, act, vec, ten, gps = self.syn, self.act, self.vec, self.ten, self.gps
        b_qk, b_ex, b_pv, b_tri = (self.sem("b_qk"), self.sem("b_ex"), self.sem("b_pv"),
                                   self.sem("b_tri"))
        b_rc, b_bc, b_fin, b_wo, b_ep = (self.sem("b_rc"), self.sem("b_bc"), self.sem("b_fin"),
                                         self.sem("b_wo"), self.sem("b_ep"))
        b_dq, b_dx, b_dy = self.sem("b_dq"), self.sem("b_dx"), self.sem("b_dy")
        dgw3 = self.sem("dgw3")
        pa = self.p2a_tot
        self.fence()
        if _DUMP:
            for pr in range(8):
                self.dmp.inc(syn.dma_start(out=self.dbg2d[pr * 128:(pr + 1) * 128, :],
                                           in_=self.k_fm[:, pr * T:(pr + 1) * T]), 16)
            self.dmp.inc(syn.dma_start(out=self.dbg3d, in_=self.v_tm[:, :]), 16)
            self.dmp.inc(syn.dma_start(out=self.dbg4d[:, 0:16], in_=self.bias_all[:, :]), 16)
            self.dmp.inc(syn.dma_start(out=self.dbg4d[:, 16:24], in_=self.biasg[:, :]), 16)
            self.dmp.inc(syn.dma_start(out=self.dbg4d[:, 24:60], in_=self.nmax[:, :]), 16)

        _st = ExitStack()
        woef = _st.enter_context(nc.sbuf_tensor("woef", [128, 4 * DM], BF16))
        qfo = _st.enter_context(nc.sbuf_tensor("qfo", [128, 8 * 2048], BF16, is_transpose_dest=True))
        pt0 = _st.enter_context(nc.sbuf_tensor("pt0", [128, 1024], BF16))
        pt1 = _st.enter_context(nc.sbuf_tensor("pt1", [128, 1024], BF16))
        oTe = _st.enter_context(nc.sbuf_tensor("oTe", [128, 4 * 1024], BF16))
        rdrow = _st.enter_context(nc.sbuf_tensor("rdrow", [1, 1024], F32))
        rbbc = _st.enter_context(nc.sbuf_tensor("rbbc", [32, 1024], F32))
        x1t2 = _st.enter_context(nc.sbuf_tensor("x1t2", [128, 2 * DM], F32))
        stt2 = _st.enter_context(nc.sbuf_tensor("stt2", [128, 2 * 4], F32))
        mlasb = _st.enter_context(nc.sbuf_tensor("mlasb", [128, 2 * DM], F32))
        y1sb = _st.enter_context(nc.sbuf_tensor("y1sb", [128, 2 * DM], F32))
        ps0 = _st.enter_context(nc.psum_tensor("ps0", [128, 1024], F32))
        ps1 = _st.enter_context(nc.psum_tensor("ps1", [128, 1024], F32))
        poT = _st.enter_context(nc.psum_tensor("poT", [128, 1024], F32))
        pbc = _st.enter_context(nc.psum_tensor("pbc", [128, 1024], F32))
        if True:

            for vc in range(4):
                dgw3.inc(gps.dma_start(out=woef[:, vc * DM:(vc + 1) * DM],
                                       in_=self.woeffd[vc * 128:(vc + 1) * 128, :]), 16)
            pts = [pt0, pt1]
            pss = [ps0, ps1]

            def emit(which, eng, role):
                qk = ex = pv = tri_n = fin = wo = ep = dx = 0
                eng.wait_ge(self._sems["v2p"].h, pa["v2p"])
                eng.wait_ge(self._sems["a2p"].h, pa["a2p"])
                eng.wait_ge(self._sems["p2p"].h, pa["p2p"])
                if which == "ten":
                    eng.wait_ge(dgw3.h, 64)
                for qc in range(2):
                    qb = QB[role][qc]
                    E = (qb + 1024) // 128
                    if which == "act":
                        if qc == 0:
                            eng.wait_ge(self._sems["d2q"].h, pa["d2q"])
                        for pr_ in range(8):
                            for tt in range(8):
                                eng.dma_start_transpose(
                                    out=qfo[:, pr_ * 2048 + qc * 1024 + tt * 128:
                                            pr_ * 2048 + qc * 1024 + (tt + 1) * 128],
                                    in_=self.qd[qb + tt * 128:qb + (tt + 1) * 128,
                                                pr_ * 128:(pr_ + 1) * 128]
                                ).then_inc(b_dq.h, 16)
                    for h in range(MH):
                        hh = 16 * qc + h
                        pr_, rb = h // 2, (h % 2) * 64
                        kts = list(range(E - 1, -1, -1))
                        first_b0 = next(kt for kt in kts if max(kt * 128 - qb, 0) < 512)
                        descs = []
                        tn_run = tri_n
                        for i, kt in enumerate(kts):
                            r = kt * 128 - qb
                            diag = (r >= 0)
                            if diag:
                                tn_run += 1
                            descs.append(dict(i=i, kt=kt, r=r, lo=max(r, 0), diag=diag,
                                              g=qk + i, tri_after=tn_run))

                        def emit_qk(d):
                            if d["g"] >= 2:
                                eng.wait_ge(b_ex.h, d["g"] - 1)
                            if qc == 1 and h == 0 and d["i"] < 2:
                                eng.wait_ge(b_ep.h, 8)
                            if d["i"] == 0:
                                eng.wait_ge(b_fin.h, hh)
                                eng.wait_ge(b_dq.h, 16 * (qc * 64 + (pr_ + 1) * 8))
                            kt = d["kt"]
                            for nn in range(2):
                                mm_ = eng.matmul(pss[d["g"] % 2][:, nn * 512:(nn + 1) * 512],
                                                 self.k_fm[rb:rb + 64,
                                                           pr_ * T + kt * 128:pr_ * T + (kt + 1) * 128],
                                                 qfo[rb:rb + 64,
                                                     pr_ * 2048 + qc * 1024 + nn * 512:
                                                     pr_ * 2048 + qc * 1024 + (nn + 1) * 512],
                                                 start=True, stop=True)
                            mm_.then_inc(b_qk.h, 1)

                        def emit_pv(d):
                            eng.wait_ge(b_ex.h, d["g"] + 1)
                            if d["diag"]:
                                eng.wait_ge(b_tri.h, d["tri_after"])
                            kt, lo = d["kt"], d["lo"]
                            segs = []
                            if lo < 512:
                                segs.append((lo, 512, 0))
                            segs.append((max(lo, 512), 1024, 1))
                            for (a, bnd, bank) in segs:
                                fw = (bank == 1 and d["i"] == 0) or (bank == 0 and kt == first_b0)
                                mm_ = eng.matmul(poT[0:33, a:bnd],
                                                 self.v_tm[:, kt * 528 + h * 33:
                                                           kt * 528 + (h + 1) * 33],
                                                 pts[d["g"] % 2][:, a:bnd],
                                                 start=fw, stop=(kt == 0))
                            mm_.then_inc(b_pv.h, 1)

                        for i, d in enumerate(descs):
                            g = d["g"]
                            lo = d["lo"]
                            if which == "ten":
                                emit_qk(d)
                                if i >= 1:
                                    emit_pv(descs[i - 1])
                                if i == len(descs) - 1:
                                    emit_pv(d)
                            if which == "act":
                                eng.wait_ge(b_qk.h, g + 1)
                                if g >= 2:
                                    eng.wait_ge(b_pv.h, g - 1)
                                eng.activation(out=pts[g % 2][:, lo:1024],
                                               in_=pss[g % 2][:, lo:1024], func=AF.Exp,
                                               scale=0.125, bias=self.bias_all[:, h:h + 1]
                                               ).then_inc(b_ex.h, 1)
                            if which == "vec" and d["diag"]:
                                eng.wait_ge(b_ex.h, g + 1)
                                eng.tensor_tensor(out=pts[g % 2][:, d["r"]:d["r"] + 128],
                                                  in0=pts[g % 2][:, d["r"]:d["r"] + 128],
                                                  in1=self.tri[:, :], op=ALU.mult
                                                  ).then_inc(b_tri.h, 1)
                            qk += 1
                            ex += 1
                            pv += 1
                            if d["diag"]:
                                tri_n += 1
                        if which == "vec":
                            eng.wait_ge(b_pv.h, pv)
                            if qc == 1 and h == 0:
                                eng.wait_ge(b_wo.h, 8)
                            eng.reciprocal(out=rdrow[0:1, :], in_=poT[32:33, :]
                                           ).then_inc(b_rc.h, 1)
                        if which == "ten":
                            eng.wait_ge(b_rc.h, hh + 1)
                            if hh >= 1:
                                eng.wait_ge(b_fin.h, hh)
                            for nn in range(2):
                                mm_ = eng.matmul(pbc[0:32, nn * 512:(nn + 1) * 512],
                                                 self.ones_f[0:1, 0:32],
                                                 rdrow[0:1, nn * 512:(nn + 1) * 512],
                                                 start=True, stop=True)
                            mm_.then_inc(b_bc.h, 1)
                        if which == "vec":
                            eng.wait_ge(b_bc.h, hh + 1)
                            eng.tensor_copy(out=rbbc[0:32, :], in_=pbc[0:32, :])
                            eng.tensor_tensor(out=oTe[(h % 4) * 32:(h % 4) * 32 + 32,
                                                      (h // 4) * 1024:(h // 4 + 1) * 1024],
                                              in0=poT[0:32, :], in1=rbbc[0:32, :],
                                              op=ALU.mult).then_inc(b_fin.h, 1)
                        fin += 1
                    for tsub in range(8):
                        eidx = 8 * qc + tsub
                        bs = eidx % 2
                        can0 = qc * 1024 + tsub * 128
                        tok = qb + tsub * 128
                        if which == "syn":
                            if eidx >= 2:
                                eng.wait_ge(b_ep.h, eidx - 1)
                            eng.dma_start(out=x1t2[:, bs * DM:(bs + 1) * DM],
                                          in_=self.x1d[tok:tok + 128, :]).then_inc(b_dx.h, 16)
                            eng.dma_start(out=stt2[:, bs * 4:bs * 4 + 3],
                                          in_=self.std_[tok:tok + 128, 0:3]).then_inc(b_dx.h, 16)
                            eng.wait_ge(b_dx.h, dx + 32)
                            eng.dma_start(out=self.x1cd[can0:can0 + 128, :],
                                          in_=x1t2[:, bs * DM:(bs + 1) * DM]).then_inc(b_dy.h, 16)
                            eng.wait_ge(b_ep.h, eidx + 1)
                            eng.dma_start(out=self.y1d[can0:can0 + 128, :],
                                          in_=y1sb[:, bs * DM:(bs + 1) * DM]).then_inc(b_dy.h, 16)
                        if which == "ten":
                            eng.wait_ge(b_fin.h, 16 * (qc + 1))
                            eng.wait_ge(b_ex.h, ex)
                            if eidx >= 2:
                                eng.wait_ge(b_ep.h, eidx - 1)
                            for fh in range(2):
                                for vc in range(4):
                                    mm_ = eng.matmul(pss[eidx % 2][:, fh * 512:(fh + 1) * 512],
                                                     oTe[:, vc * 1024 + tsub * 128:
                                                         vc * 1024 + (tsub + 1) * 128],
                                                     woef[:, vc * DM + fh * 512:
                                                          vc * DM + (fh + 1) * 512],
                                                     start=(vc == 0), stop=(vc == 3))
                            mm_.then_inc(b_wo.h, 1)
                        if which == "vec":
                            eng.wait_ge(b_wo.h, eidx + 1)
                            eng.wait_ge(b_dx.h, dx + 32)
                            eng.tensor_copy(out=mlasb[:, bs * DM:bs * DM + 512],
                                            in_=pss[eidx % 2][:, 0:512])
                            eng.tensor_copy(out=mlasb[:, bs * DM + 512:(bs + 1) * DM],
                                            in_=pss[eidx % 2][:, 512:1024])
                            eng.tensor_scalar(out=y1sb[:, bs * DM:(bs + 1) * DM],
                                              in0=x1t2[:, bs * DM:(bs + 1) * DM],
                                              scalar1=stt2[:, bs * 4:bs * 4 + 1],
                                              scalar2=stt2[:, bs * 4 + 1:bs * 4 + 2],
                                              op0=ALU.subtract, op1=ALU.mult)
                            eng.tensor_tensor(out=y1sb[:, bs * DM:(bs + 1) * DM],
                                              in0=y1sb[:, bs * DM:(bs + 1) * DM],
                                              in1=mlasb[:, bs * DM:(bs + 1) * DM],
                                              op=ALU.add).then_inc(b_ep.h, 1)
                        wo += 1
                        ep += 1
                        dx += 32
                return dict(qk=qk, ex=ex, pv=pv, tri=tri_n, fin=fin, wo=wo, ep=ep, dx=dx)

            tot = {}
            if _RFIX is not None:
                tot["t0"] = tot["t1"] = emit("ten", ten, _RFIX)
                tot["a0"] = tot["a1"] = emit("act", act, _RFIX)
                tot["v0"] = tot["v1"] = emit("vec", vec, _RFIX)
                tot["s0"] = tot["s1"] = emit("syn", syn, _RFIX)
            if _RFIX is None:
              with nc.Block() as block:
                  @block.tensor
                  def _(eng):
                      with eng.register("role_t") as r:
                          self.ds.wait(eng)
                          eng.reg_load(r, self.srole[0:1, 0:1])
                          with eng.If_eq(r, 0):
                              tot["t0"] = emit("ten", eng, 0)
                          with eng.If_eq(r, 1):
                              tot["t1"] = emit("ten", eng, 1)

                  @block.scalar
                  def _(eng):
                      with eng.register("role_a") as r:
                          self.ds.wait(eng)
                          eng.reg_load(r, self.srole[0:1, 0:1])
                          with eng.If_eq(r, 0):
                              tot["a0"] = emit("act", eng, 0)
                          with eng.If_eq(r, 1):
                              tot["a1"] = emit("act", eng, 1)

                  @block.vector
                  def _(eng):
                      with eng.register("role_v") as r:
                          self.ds.wait(eng)
                          eng.reg_load(r, self.srole[0:1, 0:1])
                          with eng.If_eq(r, 0):
                              tot["v0"] = emit("vec", eng, 0)
                          with eng.If_eq(r, 1):
                              tot["v1"] = emit("vec", eng, 1)

                  @block.sync
                  def _(eng):
                      with eng.register("role_s") as r:
                          self.ds.wait(eng)
                          eng.reg_load(r, self.srole[0:1, 0:1])
                          with eng.If_eq(r, 0):
                              tot["s0"] = emit("syn", eng, 0)
                          with eng.If_eq(r, 1):
                              tot["s1"] = emit("syn", eng, 1)

            assert tot["t0"] == tot["t1"] == tot["a0"] == tot["a1"] == tot["v0"] == tot["v1"] \
                == tot["s0"] == tot["s1"], tot
            t = tot["t0"]
            b_qk.n = b_ex.n = b_pv.n = t["qk"]
            b_tri.n = t["tri"]
            b_rc.n = b_bc.n = b_fin.n = t["fin"]
            b_wo.n = b_ep.n = t["wo"]
            b_dq.n = 16 * 128
            b_dx.n = t["dx"]
            b_dy.n = 32 * t["ep"]
        _st.close()

    def p3(self):
        nc = self.nc
        syn, act, vec, ten, gps = self.syn, self.act, self.vec, self.ten, self.gps
        stat, cst, wln = self.stat, self.cst, self.wln
        BL, NBL = 512, 4
        d3 = self.sem("d3")
        d3w = self.sem("d3w")
        d3t = self.sem("d3t")
        dout = self.sem("dout")
        p3p, v3p, a3p = self.sem("p3p"), self.sem("v3p"), self.sem("a3p")
        p3d = self.sem("p3d")
        b_ep, b_dy = self._sems["b_ep"], self._sems["b_dy"]
        self.fence()

        _st = ExitStack()
        y1t = _st.enter_context(nc.sbuf_tensor("y1t", [128, 4 * DM], F32))
        yn2tm = _st.enter_context(nc.sbuf_tensor("yn2tm", [128, DM], BF16))
        yn2fm = _st.enter_context(nc.sbuf_tensor("yn2fm", [128, 8 * BL], BF16, is_transpose_dest=True))
        wgt = _st.enter_context(nc.sbuf_tensor("wgt", [128, 8 * 8], BF16))
        lg = _st.enter_context(nc.sbuf_tensor("lg", [128, 8], F32))
        ge2 = _st.enter_context(nc.sbuf_tensor("ge2", [128, 8], F32))
        cmb = _st.enter_context(nc.sbuf_tensor("cmb", [128, 8], F32))
        cfm = _st.enter_context(nc.sbuf_tensor("cfm", [1, 8 * BL], F32))
        cbc = _st.enter_context(nc.sbuf_tensor("cbc", [128, 8 * BL], BF16))
        wgu_sb = _st.enter_context(nc.sbuf_tensor("wgu_sb", [128, 2 * 8 * DM], BF16))
        wd_sb = _st.enter_context(nc.sbuf_tensor("wd_sb", [128, 2 * 4 * DM], BF16))
        gsl = _st.enter_context(nc.sbuf_tensor("gsl", [128, 4 * 512], BF16))
        ubf = _st.enter_context(nc.sbuf_tensor("ubf", [128, 4 * 512], BF16))
        gup = _st.enter_context(nc.sbuf_tensor("gup", [128, 9 * 4 * BL], BF16))
        y2s = _st.enter_context(nc.sbuf_tensor("y2s", [128, 2 * DM], F32))
        x1c2 = _st.enter_context(nc.sbuf_tensor("x1c2", [128, 4 * DM], F32))
        osb = _st.enter_context(nc.sbuf_tensor("osb", [128, 4 * DM], F32))
        sq3 = _st.enter_context(nc.sbuf_tensor("sq3", [128, DM], BF16))
        pga = _st.enter_context(nc.psum_tensor("pga", [128, 2048], F32))
        pgb = _st.enter_context(nc.psum_tensor("pgb", [128, 2048], F32))
        if True:

            for ci in range(8):
                d3w.inc(gps.dma_start(out=wgt[:, ci * 8:(ci + 1) * 8],
                                      in_=self.wgated[ci * 128:(ci + 1) * 128, :]), 16)

            for bl in range(NBL):
                c0 = bl * BL
                b_dy.wait(syn)
                if bl >= 1:
                    v3p.wait(syn, v3p.n)
                    dout.wait(vec)
                for s in range(4):
                    d3.inc(syn.dma_start(out=y1t[:, s * DM:(s + 1) * DM],
                                         in_=self.y1d[c0 + s * 128:c0 + (s + 1) * 128, :]), 16)
                for s in range(4):
                    d3.wait(act)
                    if bl >= 1 and s == 0:
                        p3p.wait(act, p3p.n)
                    act.activation(out=sq3[:, :], in_=y1t[:, s * DM:(s + 1) * DM],
                                   func=AF.Square, accum_out=stat[:, 0:1])
                    act.activation(out=stat[:, 1:2], in_=stat[:, 0:1], func=AF.Ln,
                                   scale=1.0 / DM, bias=cst[:, 0:1])
                    a3p.inc(act.activation(out=stat[:, 1:2], in_=stat[:, 1:2], func=AF.Exp,
                                           scale=-0.5))
                    a3p.wait(vec)
                    v3p.inc(vec.tensor_scalar(out=yn2tm[:, :], in0=y1t[:, s * DM:(s + 1) * DM],
                                              scalar1=stat[:, 1:2], scalar2=None, op0=ALU.mult))
                    v3p.wait(act)
                    for ci in range(8):
                        d3t.inc(act.dma_start_transpose(
                            out=yn2fm[:, ci * BL + s * 128:ci * BL + (s + 1) * 128],
                            in_=yn2tm[:, ci * 128:(ci + 1) * 128]), 16)
                d3t.wait(vec)
                for ci in range(8):
                    vec.tensor_scalar(out=yn2fm[:, ci * BL:(ci + 1) * BL],
                                      in0=yn2fm[:, ci * BL:(ci + 1) * BL],
                                      scalar1=wln[:, 8 + ci:9 + ci], scalar2=None, op0=ALU.mult)
                v3p.inc(self.vflush())
                for s in range(4):
                    v3p.wait(ten)
                    if bl == 0 and s == 0:
                        d3w.wait(ten)
                    for ci in range(8):
                        ten.matmul(pga[:, 0:8],
                                   yn2fm[:, ci * BL + s * 128:ci * BL + (s + 1) * 128],
                                   wgt[:, ci * 8:(ci + 1) * 8],
                                   start=(ci == 0), stop=False)
                    mm_ = ten.matmul(pga[:, 0:8], self.ones_f[0:1, :],
                                     self.b8row[0:1, :], start=False, stop=True)
                    p3p.inc(mm_)
                    p3p.wait(vec)
                    vec.tensor_copy(out=lg[:, :], in_=pga[:, 0:8])
                    self.vflush()
                    vec.reduce_max(out=stat[:, 0:1], in_=lg[:, :], axis=AX.X)
                    self.vflush()
                    vec.tensor_scalar(out=cmb[:, :], in0=lg[:, :], scalar1=stat[:, 0:1],
                                      scalar2=1e9, op0=ALU.is_ge, op1=ALU.mult)
                    self.vflush()
                    vec.tensor_tensor(out=cmb[:, :], in0=lg[:, :], in1=cmb[:, :],
                                      op=ALU.subtract)
                    self.vflush()
                    vec.reduce_max(out=stat[:, 1:2], in_=cmb[:, :], axis=AX.X)
                    self.vflush()
                    vec.tensor_tensor(out=stat[:, 2:3], in0=stat[:, 0:1], in1=stat[:, 1:2],
                                      op=ALU.add)
                    self.vflush()
                    v3p.inc(vec.tensor_scalar(out=cmb[:, :], in0=lg[:, :], scalar1=2.0,
                                              scalar2=stat[:, 2:3], op0=ALU.mult,
                                              op1=ALU.subtract))
                    v3p.wait(act)
                    a3p.inc(act.activation(out=cmb[:, :], in_=cmb[:, :], func=AF.Sigmoid))
                    a3p.wait(vec)
                    vec.tensor_scalar(out=ge2[:, :], in0=lg[:, :], scalar1=stat[:, 1:2],
                                      scalar2=None, op0=ALU.is_ge)
                    self.vflush()
                    v3p.inc(vec.tensor_tensor(out=cmb[:, :], in0=cmb[:, :], in1=ge2[:, :],
                                              op=ALU.mult))
                    v3p.wait(ten)
                    for e in range(8):
                        mm_ = ten.transpose(pgb[0:1, e * 128:(e + 1) * 128], cmb[:, e:e + 1],
                                            self.idf[:, :])
                    p3p.inc(mm_)
                    p3p.wait(vec)
                    v3p.inc(vec.tensor_copy(
                        out=cfm.ap().rearrange("p (e c) -> p e c", c=BL)[:, :, s * 128:(s + 1) * 128],
                        in_=pgb[0:1, 0:1024].rearrange("p (e c) -> p e c", c=128)))
                for e in range(8):
                    v3p.wait(ten)
                    mm_ = ten.matmul(pga[:, 0:512], self.ones_f[0:1, :],
                                     cfm[0:1, e * BL:(e + 1) * BL], start=True, stop=True)
                    p3p.inc(mm_)
                    p3p.wait(vec)
                    v3p.inc(vec.tensor_copy(out=cbc[:, e * BL:(e + 1) * BL], in_=pga[:, 0:512]))
                guland = {}
                for e in range(9):
                    we = e % 2
                    if e >= 2:
                        p3p.wait(gps, guland[e - 2])
                    elif bl >= 1:
                        p3p.wait(gps, self._p3_guland_last)
                    for ci in range(8):
                        d3w.inc(gps.dma_start(
                            out=wgu_sb[:, we * 8 * DM + ci * DM:we * 8 * DM + (ci + 1) * DM],
                            in_=self.wgud[e * DM + ci * 128:e * DM + (ci + 1) * 128, :]), 16)
                    d3w.wait(ten)
                    if e == 0:
                        v3p.wait(ten)
                    if bl + e >= 1:
                        v3p.wait(ten, v3p.n)
                    for ic in range(4):
                        for ci in range(8):
                            ten.matmul(pga[:, ic * 512:(ic + 1) * 512],
                                       wgu_sb[:, we * 8 * DM + ci * DM + ic * 128:
                                              we * 8 * DM + ci * DM + (ic + 1) * 128],
                                       yn2fm[:, ci * BL:(ci + 1) * BL],
                                       start=(ci == 0), stop=(ci == 7))
                    for ic in range(4):
                        for ci in range(8):
                            mm_ = ten.matmul(pgb[:, ic * 512:(ic + 1) * 512],
                                             wgu_sb[:, we * 8 * DM + ci * DM + 512 + ic * 128:
                                                    we * 8 * DM + ci * DM + 512 + (ic + 1) * 128],
                                             yn2fm[:, ci * BL:(ci + 1) * BL],
                                             start=(ci == 0), stop=(ci == 7))
                    p3p.inc(mm_)
                    p3p.wait(act)
                    for ic in range(4):
                        act.activation(out=gsl[:, ic * 512:(ic + 1) * 512],
                                       in_=pga[:, ic * 512:(ic + 1) * 512], func=AF.Silu)
                    a3p.inc(self.aflush())
                    a3p.wait(vec)
                    p3p.wait(vec)
                    for ic in range(4):
                        vec.tensor_copy(out=ubf[:, ic * 512:(ic + 1) * 512],
                                        in_=pgb[:, ic * 512:(ic + 1) * 512])
                    for ic in range(4):
                        vec.tensor_tensor(out=gsl[:, ic * 512:(ic + 1) * 512],
                                          in0=gsl[:, ic * 512:(ic + 1) * 512],
                                          in1=ubf[:, ic * 512:(ic + 1) * 512], op=ALU.mult)
                    if e < 8:
                        for ic in range(4):
                            vec.tensor_tensor(out=gsl[:, ic * 512:(ic + 1) * 512],
                                              in0=gsl[:, ic * 512:(ic + 1) * 512],
                                              in1=cbc[:, e * BL:(e + 1) * BL], op=ALU.mult)
                    for ic in range(4):
                        vec.tensor_copy(out=gup[:, e * 4 * BL + ic * BL:e * 4 * BL + (ic + 1) * BL],
                                        in_=gsl[:, ic * 512:(ic + 1) * 512])
                    v3p.inc(self.vflush())
                    guland[e] = p3p.n
                self._p3_guland_last = p3p.n
                vland_dn = v3p.n
                for e in range(9):
                    we = e % 2
                    if p3d.n >= 2:
                        p3d.wait(gps, p3d.n - 1)
                    for ic in range(4):
                        d3w.inc(gps.dma_start(
                            out=wd_sb[:, we * 4 * DM + ic * DM:we * 4 * DM + (ic + 1) * DM],
                            in_=self.wdd[e * EI + ic * 128:e * EI + (ic + 1) * 128, :]), 16)
                    d3w.wait(ten)
                    if e == 0:
                        v3p.wait(ten, vland_dn)
                    for s in range(4):
                        for fh in range(2):
                            slot = s * 2 + fh
                            pd = pga if slot < 4 else pgb
                            so = (slot % 4) * 512
                            for ic in range(4):
                                mm_ = ten.matmul(pd[:, so:so + 512],
                                                 gup[:, e * 4 * BL + ic * BL + s * 128:
                                                     e * 4 * BL + ic * BL + (s + 1) * 128],
                                                 wd_sb[:, we * 4 * DM + ic * DM + fh * 512:
                                                      we * 4 * DM + ic * DM + (fh + 1) * 512],
                                                 start=(e == 0 and ic == 0), stop=(e == 8 and ic == 3))
                    p3d.inc(mm_)
                for s in range(4):
                    bs = s % 2
                    d3.inc(syn.dma_start(out=x1c2[:, s * DM:(s + 1) * DM],
                                         in_=self.x1cd[c0 + s * 128:c0 + (s + 1) * 128, :]), 16)
                    p3p.wait(vec)
                    p3d.wait(vec)
                    d3.wait(vec)
                    pd0 = pga if s * 2 < 4 else pgb
                    pd1 = pga if s * 2 + 1 < 4 else pgb
                    vec.tensor_copy(out=y2s[:, bs * DM:bs * DM + 512],
                                    in_=pd0[:, ((s * 2) % 4) * 512:((s * 2) % 4) * 512 + 512])
                    vec.tensor_copy(out=y2s[:, bs * DM + 512:(bs + 1) * DM],
                                    in_=pd1[:, ((s * 2 + 1) % 4) * 512:((s * 2 + 1) % 4) * 512 + 512])
                    v3p.inc(vec.tensor_tensor(out=y2s[:, bs * DM:(bs + 1) * DM],
                                              in0=y2s[:, bs * DM:(bs + 1) * DM],
                                              in1=y1t[:, s * DM:(s + 1) * DM], op=ALU.add))
                    v3p.wait(act)
                    act.activation(out=sq3[:, :], in_=y2s[:, bs * DM:(bs + 1) * DM],
                                   func=AF.Square, accum_out=stat[:, 0:1])
                    a3p.inc(act.activation(out=sq3[:, :], in_=y2s[:, bs * DM:(bs + 1) * DM],
                                           func=AF.Copy, accum_out=stat[:, 1:2]))
                    a3p.wait(vec)
                    vec.tensor_scalar(out=stat[:, 1:2], in0=stat[:, 1:2], scalar1=1.0 / DM,
                                      scalar2=None, op0=ALU.mult)
                    vec.tensor_scalar(out=stat[:, 0:1], in0=stat[:, 0:1], scalar1=1.0 / DM,
                                      scalar2=None, op0=ALU.mult)
                    vec.tensor_tensor(out=stat[:, 2:3], in0=stat[:, 1:2], in1=stat[:, 1:2],
                                      op=ALU.mult)
                    self.vflush()
                    v3p.inc(vec.tensor_tensor(out=stat[:, 0:1], in0=stat[:, 0:1],
                                              in1=stat[:, 2:3], op=ALU.subtract))
                    v3p.wait(act)
                    act.activation(out=stat[:, 2:3], in_=stat[:, 0:1], func=AF.Ln,
                                   bias=cst[:, 0:1])
                    a3p.inc(act.activation(out=stat[:, 2:3], in_=stat[:, 2:3], func=AF.Exp,
                                           scale=-0.5))
                    a3p.wait(vec)
                    vec.tensor_scalar(out=osb[:, s * DM:(s + 1) * DM],
                                      in0=y2s[:, bs * DM:(bs + 1) * DM],
                                      scalar1=stat[:, 1:2], scalar2=stat[:, 2:3],
                                      op0=ALU.subtract, op1=ALU.mult)
                    v3p.inc(vec.tensor_tensor(out=osb[:, s * DM:(s + 1) * DM],
                                              in0=osb[:, s * DM:(s + 1) * DM],
                                              in1=x1c2[:, s * DM:(s + 1) * DM], op=ALU.add))
                    v3p.wait(syn)
                    dout.inc(syn.dma_start(out=self.outd[c0 + s * 128:c0 + (s + 1) * 128, :],
                                           in_=osb[:, s * DM:(s + 1) * DM]), 16)
            dout.wait(syn)
        _st.close()


_NC_CACHE = {}


def _get_nc():
    if "nc" not in _NC_CACHE:
        _NC_CACHE["nc"] = Builder().build()
    return _NC_CACHE["nc"]


def kernel(**inputs):
    x = np.asarray(inputs["L0_attn_x"], np.float32)
    w = _stage_weights(inputs)
    c = _host_consts()
    common = {
        "wqkvg": w["wqkvg"], "wo_vf": w["wo_vf"], "wq_mla": w["wq_mla"], "wkva": w["wkva"],
        "wkvb": w["wkvb"], "woeff": w["woeff"], "wgate": w["wgate"], "bias8": w["bias8"],
        "wgu": w["wgu"], "wd": w["wd"], "wln1": w["wln1"], "wln2": w["wln2"], "gnw": w["gnw"],
        "cosfm": c["cosfm"], "sinfm": c["sinfm"], "maskt": c["maskt"], "cd": c["cd"],
        "kdec": c["kdec"], "mcosq": c["mcosq"], "msinq": c["msinq"], "mcosk": c["mcosk"],
        "msink": c["msink"], "tri": c["tri"], "idb": c["idb"], "idf": c["idf"],
    }
    in_maps = []
    for core in range(8):
        seq, role = core // 2, core % 2
        m = dict(common)
        m["x0"] = np.ascontiguousarray(x[seq])
        m["role"] = np.array([[role]], dtype=np.int32)
        in_maps.append(m)
    nc = _get_nc()
    res = run_bass_kernel_spmd(nc, in_maps, core_ids=list(range(8)))
    out = np.empty((4, T, DM), dtype=np.float32)
    for core in range(8):
        seq, role = core // 2, core % 2
        o = np.asarray(res.results[core]["out"])
        if role == 0:
            out[seq, 0:1024] = o[0:1024]
            out[seq, 3072:4096] = o[1024:2048]
        else:
            out[seq, 1024:3072] = o[0:2048]
    return out


if __name__ == "__main__":
    b = Builder()
    nc = b.build()
    n = sum(1 for _ in nc.all_instructions())
    print("build ok, instructions:", n)
